# version 36
# baseline (speedup 1.0000x reference)
import numpy as np
import ml_dtypes
import concourse.bass as bass
import concourse.mybir as mybir
import concourse.bass_utils as _bass_utils
from concourse.bass_utils import run_bass_kernel_spmd

_orig_run_command = _bass_utils.run_command


def _run_command_ldwopt(cmd, *a, **kw):
    if isinstance(cmd, list):
        cmd = ["--enable-ldw-opt=true" if c == "--enable-ldw-opt=false" else c
               for c in cmd]
    return _orig_run_command(cmd, *a, **kw)


_bass_utils.run_command = _run_command_ldwopt
from contextlib import ExitStack

F32 = mybir.dt.float32
BF16 = mybir.dt.bfloat16
I32 = mybir.dt.int32
ALU = mybir.AluOpType
AF = mybir.ActivationFunctionType
AX = mybir.AxisListType
BFnp = ml_dtypes.bfloat16

NCORES = 8
S, D = 16384, 1024
T = S // NCORES
NCH = T // 128
NHC = NCH + 1
NQB = T // 256
NH = 16
LH = 4
FFI = 2816
MAGIC = float(np.float32(3 * 2 ** 21))
C_EXP = 0x7F800000
C_RP = 0x7F000000


class Plan:
    ENGINES = ("sync", "scalar", "vector", "tensor", "gpsimd")

    def __init__(self):
        self.ops = []
        self.cnt = {}
        self.res = {}
        self.sems = set()

    def _sem(self, name):
        if name not in self.cnt:
            self.cnt[name] = 0
            self.sems.add(name)
        return name

    def op(self, eng, emit, reads=(), writes=(), sem=None, inc=1,
           serial=False):
        sem = self._sem(sem or ("s_" + eng))
        waits = {}

        def need(psem, v, peng):
            if v <= 0:
                return
            if peng == "tensor" and eng == "tensor":
                return
            waits[psem] = max(waits.get(psem, 0), v)

        for r in reads:
            rec = self.res.get(r)
            if rec and rec["w"]:
                need(*rec["w"])
        for w in writes:
            rec = self.res.get(w)
            if rec:
                if rec["w"]:
                    need(*rec["w"])
                for dep in rec["r"]:
                    need(*dep)
        self.cnt[sem] += inc
        me = (sem, self.cnt[sem], eng)
        for r in reads:
            self.res.setdefault(r, {"w": None, "r": []})["r"].append(me)
        for w in writes:
            self.res[w] = {"w": me, "r": []}
        post_self = (sem, self.cnt[sem]) if serial else None
        self.ops.append((eng, emit, waits, sem, inc, post_self))
        return me

    def rebind(self, resources, token):
        w = self.res[token]["w"]
        for r in resources:
            self.res[r] = {"w": w, "r": []}

    def fence(self, eng, sem, name):
        sem = self._sem(sem)
        fs = self._sem("f_" + name)
        cur = self.cnt[sem]
        self.cnt[fs] += 1
        fv = self.cnt[fs]

        def emit(e, sems):
            e.wait_ge(sems[sem], cur)
            e.sem_inc(sems[fs], 1)

        tok = "tok_" + name + str(fv)
        self.res[tok] = {"w": (fs, fv, eng), "r": []}
        self.ops.append((eng, ("RAW", emit), {}, None, 0, None))
        return tok

    def barrier(self):
        snap = dict(self.cnt)
        for eng in self.ENGINES:
            def emit(e, sems, snap=snap):
                for s_, v in sorted(snap.items()):
                    if v > 0:
                        e.wait_ge(sems[s_], v)
            self.ops.append((eng, ("RAW", emit), {}, None, 0, None))

    def emit_all(self, block, sems):
        per = {e: [] for e in self.ENGINES}
        for rec in self.ops:
            per[rec[0]].append(rec)

        def run_engine(e_obj, eng):
            hi = {}
            for (_, emit, waits, sem, inc, post_self) in per[eng]:
                if isinstance(emit, tuple) and emit[0] == "RAW":
                    emit[1](e_obj, sems)
                    continue
                for wsem, wval in sorted(waits.items()):
                    if hi.get(wsem, 0) < wval:
                        e_obj.wait_ge(sems[wsem], wval)
                        hi[wsem] = wval
                ins = emit(e_obj)
                if sem is not None:
                    ins.then_inc(sems[sem], inc)
                if post_self is not None:
                    e_obj.wait_ge(sems[post_self[0]], post_self[1])
                    hi[post_self[0]] = max(hi.get(post_self[0], 0),
                                           post_self[1])

        block.sync(lambda e: run_engine(e, "sync"))
        block.scalar(lambda e: run_engine(e, "scalar"))
        block.vector(lambda e: run_engine(e, "vector"))
        block.tensor(lambda e: run_engine(e, "tensor"))
        block.gpsimd(lambda e: run_engine(e, "gpsimd"))


def make_local_mask(first_block_core0):
    kk = np.arange(384)[:, None]
    qi = np.arange(256)[None, :]
    m = (kk >= qi) & (kk <= qi + 127) & (kk >= 1)
    if first_block_core0:
        m &= kk >= 128
    out = np.zeros((128, 768), dtype=BFnp)
    for c3 in range(3):
        out[:, c3 * 256:(c3 + 1) * 256] = m[c3 * 128:(c3 + 1) * 128].astype(BFnp)
    return out


def make_lat_mask(core):
    tok = np.arange(T) + core * T
    cid = tok // 256
    return (np.arange(64)[:, None] < cid[None, :]).astype(BFnp)


def build_program():
    nc = bass.Bass("TRN2", target_bir_lowering=False)
    P = Plan()

    def din(name, shape, dt):
        return nc.dram_tensor(name, list(shape), dt, kind="ExternalInput").ap()

    x_d = din("x", [NHC * 128, D], F32)
    qkvw_d = din("qkv_w", [D, 3 * D], F32)
    rqw_d = din("rq_w", [D, 256], F32)
    rkvw_d = din("rkv_w", [D, 512], F32)
    routw_d = din("rout_w", [256, D], F32)
    outw_d = din("out_w", [D, D], F32)
    glc_d = din("gl_col", [128, 1], F32)
    grc_d = din("gr2", [128, 2], F32)
    qb_d = din("qb_fm", [128, 8], F32)
    kb_d = din("kb_fm", [128, 8], F32)
    vb_d = din("vb_row", [1, D], F32)
    rqb_d = din("rqb_fm", [128, 2], F32)
    rkvb_d = din("rkvb_fm", [128, 4], F32)
    routb_d = din("routb_fm", [128, 8], F32)
    outb_d = din("outb_row", [1, D], F32)
    wg_sh_d = din("wg_sh", [352, D], F32)
    wu_sh_d = din("wu_sh", [352, D], F32)
    wd_sh_d = din("wd_sh", [128, FFI], F32)
    mask0_d = din("mask0", [128, 768], BF16)
    maskg_d = din("maskg", [128, 768], BF16)
    masklat_d = din("mask_lat", [64, T], BF16)
    ident_d = din("ident", [128, 128], BF16)

    y_d = nc.dram_tensor("y", [T, D], F32, kind="ExternalOutput").ap()
    xq_d = nc.dram_tensor("xq_scratch", [128, NCH * 1024], BF16,
                          kind="Internal").ap()
    wgu_c = nc.dram_tensor("wgu_c", [2048, 352], BF16, kind="Internal").ap()
    wgu_g = nc.dram_tensor("wgu_g", [2048 * NCORES, 352], BF16,
                           kind="Internal").ap()
    wd_c = nc.dram_tensor("wd_c", [FFI, 128], BF16, kind="Internal").ap()
    wd_g = nc.dram_tensor("wd_g", [FFI * NCORES, 128], BF16,
                          kind="Internal").ap()
    rkv_c = nc.dram_tensor("rkv_c", [8, 512], BF16, kind="Internal").ap()
    rkv_g = nc.dram_tensor("rkv_g", [64, 512], BF16, kind="Internal").ap()
    RG = [list(range(NCORES))]

    es = ExitStack()

    def sb(name, shape, dt, **kw):
        return es.enter_context(nc.sbuf_tensor(name, list(shape), dt, **kw))

    ident = sb("identsb", [128, 128], BF16)
    cvec = sb("cvec", [128, 3], F32)
    ones_r = sb("ones_r", [1, 128], BF16)
    pool = [sb(f"qp{i}", [128, 1024], F32) for i in range(3)]
    sc_a = sb("qsA", [128, 176], F32)
    sc_s = sb("qsS", [128, 176], F32)
    sc_r = sb("qsR", [128, 176], F32)
    gl_col = sb("glcolsb", [128, 1], BF16)
    gr2 = sb("gr2sb", [128, 2], BF16)
    qb_fm = sb("qbsb", [128, 8], F32)
    kb_fm = sb("kbsb", [128, 8], F32)
    vb_row = sb("vbsb", [1, D], BF16)
    rqb_fm = sb("rqbsb", [128, 2], F32)
    rkvb_fm = sb("rkvbsb", [128, 4], F32)
    routb_fm = sb("routbsb", [128, 8], F32)
    outb_row = sb("outbsb", [1, D], BF16)

    psE = es.enter_context(nc.psum_tensor("psE", [128, 1024], F32))
    psO = es.enter_context(nc.psum_tensor("psO", [128, 1024], F32))
    p4 = es.enter_context(nc.psum_tensor("p4", [128, 512], F32))
    p5 = es.enter_context(nc.psum_tensor("p5", [128, 512], F32))
    ptr = es.enter_context(nc.psum_tensor("ptr", [128, 2048], BF16))

    def dma(q, out, in_, reads, writes, serial=True):
        P.op(q, lambda e, o=out, i=in_: e.dma_start(out=o, in_=i),
             reads=reads, writes=writes, sem="d_" + q, inc=16, serial=serial)

    def cp_s(out, in_, reads, writes):
        P.op("scalar", lambda e, o=out, i=in_: e.copy(o, i),
             reads=reads, writes=writes)

    def cp_v(out, in_, reads, writes):
        P.op("vector", lambda e, o=out, i=in_: e.tensor_copy(o, i),
             reads=reads, writes=writes)

    def act(out, in_, func, reads, writes, bias=0.0, scale=1.0):
        P.op("scalar",
             lambda e, o=out, i=in_, f=func, b=bias, s=scale:
             e.activation(o, i, f, bias=b, scale=s),
             reads=reads, writes=writes)

    def tt(out, a, b, op, reads, writes):
        P.op("vector",
             lambda e, o=out, x=a, y=b, p=op: e.tensor_tensor(o, x, y, p),
             reads=reads, writes=writes)

    def mm(out, lhsT, rhs, start, stop, reads, writes, tp=None):
        P.op("tensor",
             lambda e, o=out, l=lhsT, r=rhs, st=start, sp=stop, t=tp:
             e.matmul(o, l, r, start=st, stop=sp, tile_position=t),
             reads=reads, writes=writes)

    def trans(out, in_, idn, reads, writes):
        P.op("tensor",
             lambda e, o=out, i=in_, d=idn:
             e.matmul(o, i, d, start=True, stop=True),
             reads=reads, writes=writes)

    def memset(ap_, val, writes, reads=()):
        P.op("vector", lambda e, a=ap_, v=val: e.memset(a, v), reads=reads,
             writes=writes)

    def quant(src, dst, p, w, reads, writes):
        nb = w // 16
        b0, b1, b2 = (x[0:p, 0:w] for x in pool)
        A, Ssc, R = sc_a[0:p, 0:nb], sc_s[0:p, 0:nb], sc_r[0:p, 0:nb]
        r_src = tuple(reads)
        P.op("vector", lambda e, o=A, i=src: e.tensor_reduce(
            o, i.rearrange("p (b w) -> p b w", w=16), AX.X, ALU.max,
            apply_absolute_value=True), reads=r_src, writes=("qsA",))
        P.op("vector", lambda e, o=Ssc, i=A: e.tensor_scalar(
            o, i, float(np.float32(1.0) / np.float32(6.0)), 1e-12,
            ALU.mult, ALU.max),
            reads=("qsA",), writes=("qsS",))
        P.op("vector", lambda e, o=R, i=Ssc: e.reciprocal(o, i),
             reads=("qsS",), writes=("qsR",))
        rbc = R.unsqueeze(2).broadcast_to([p, nb, 16])
        sbc = Ssc.unsqueeze(2).broadcast_to([p, nb, 16])
        v3 = src.rearrange("p (b w) -> p b w", w=16)
        P.op("vector", lambda e, o=b0.rearrange("p (b w) -> p b w", w=16),
             i=v3, r=rbc: e.tensor_tensor(o, i, r, ALU.mult),
             reads=r_src + ("qsR",), writes=("qp0",))
        P.op("vector", lambda e, o=b1.bitcast(I32), i=b0.bitcast(I32):
             e.tensor_scalar(o, i, C_EXP, None, ALU.bitwise_and),
             reads=("qp0",), writes=("qp1",))
        P.op("vector", lambda e, o=b1, i=b1: e.tensor_scalar(
            o, i, 1.0, 4.0, ALU.max, ALU.min),
            reads=("qp1",), writes=("qp1",))
        P.op("vector", lambda e, o=b2.bitcast(I32), i=b1.bitcast(I32):
             e.tensor_scalar(o, i, C_RP, -1, ALU.subtract, ALU.mult),
             reads=("qp1",), writes=("qp2",))
        P.op("vector", lambda e, o=b2, i=b0, r=b2: e.tensor_tensor(
            o, i, r, ALU.mult), reads=("qp0", "qp2"), writes=("qp2",))
        P.op("vector", lambda e, o=b0, i=b2: e.tensor_scalar(
            o, i, MAGIC, MAGIC, ALU.add, ALU.subtract),
            reads=("qp2",), writes=("qp0",))
        P.op("vector", lambda e, o=b2, i=b0, r=b1: e.tensor_tensor(
            o, i, r, ALU.mult), reads=("qp0", "qp1"), writes=("qp2",))
        P.op("vector", lambda e, o=dst.rearrange("p (b w) -> p b w", w=16),
             i=b2.rearrange("p (b w) -> p b w", w=16), r=sbc:
             e.tensor_tensor(o, i, r, ALU.mult),
             reads=("qp2", "qsS"), writes=tuple(writes))

    dma("sync", ident[:], ident_d, (), ("c.ident",))
    memset(ones_r[:], 1.0, ("c.ones",))
    memset(cvec[:, 0:1], 1e-5, ("c.cv0",))
    memset(cvec[:, 1:2], 1e-30, ("c.cv1",))
    memset(cvec[:, 2:3], 1.0, ("c.cv2",))
    dma("gpsimd", gl_col[:], glc_d, (), ("c.gl",))
    dma("gpsimd", gr2[:], grc_d, (), ("c.gr",))
    dma("sync", qb_fm[:], qb_d, (), ("c.qb",))
    dma("sync", kb_fm[:], kb_d, (), ("c.kb",))
    dma("gpsimd", vb_row[:], vb_d, (), ("c.vb",))
    dma("sync", rqb_fm[:], rqb_d, (), ("c.rqb",))
    dma("sync", rkvb_fm[:], rkvb_d, (), ("c.rkvb",))
    dma("sync", routb_fm[:], routb_d, (), ("c.routb",))
    dma("gpsimd", outb_row[:], outb_d, (), ("c.outb",))

    esB = ExitStack()

    def sbB(name, shape, dt, **kw):
        return esB.enter_context(nc.sbuf_tensor(name, list(shape), dt, **kw))

    x_roll = sbB("x_roll", [128, 2 * 1024], F32)
    x_res = sbB("x_res", [128, 1024], F32)
    xn_fm = sbB("xn_fm", [128, 8 * NHC * 128], BF16)
    xn_tm = [x_roll[:, 0:1024], x_roll[:, 1024:2048]]
    xn_bf = [sbB(f"xn_bf{i}", [128, 1024], BF16) for i in range(2)]
    xq_tm = xn_bf
    mv = [sbB(f"mv{i}", [128, 2], F32) for i in range(2)]
    bnst = [sbB(f"bnst{i}", [128, 12], F32) for i in range(2)]
    rstd = [sbB(f"rstd{i}", [128, 1], F32) for i in range(2)]
    masks = sbB("masks", [128, 2 * 768], BF16)
    masklat = sbB("masklat", [64, T], BF16)
    pooled_bf = sbB("pooled_bf", [128, 64], BF16)
    rk_fm = sbB("rk_fm", [128, 128], BF16)
    rv_l = sbB("rv_l", [64, 4 * 65], BF16)
    rkv_bf = sbB("rkv_bf", [128, 32], BF16)
    xqT_st = sbB("xqT_st", [128, 1024], BF16)

    dma("sync", masks[:, 0:768], mask0_d, (), ("c.mask0",))
    dma("sync", masks[:, 768:1536], maskg_d, (), ("c.maskg",))
    dma("sync", masklat[:], masklat_d, (), ("c.masklat",))

    esSh = ExitStack()
    w_st = esSh.enter_context(nc.sbuf_tensor("w_st", [128, 1024], F32))
    shq = esSh.enter_context(nc.sbuf_tensor("shq", [128, 1024], BF16))
    trst = esSh.enter_context(nc.sbuf_tensor("trst", [128, 1024], BF16))

    ptrF32 = ptr[:].bitcast(F32)

    def PTR(sl, w=128, p=128):
        s = (sl % 2) * 512
        return ptrF32[0:p, s:s + w]

    for wsrc, drow0 in ((wg_sh_d, 0), (wu_sh_d, 1024)):
        for rt, rows_n in enumerate((128, 128, 96)):
            dma("sync", w_st[0:rows_n, :], wsrc[rt * 128:rt * 128 + rows_n, :],
                (), ("w_st",))
            quant(w_st[0:rows_n, :], shq[0:rows_n, :], rows_n, 1024,
                  ("w_st",), ("shq",))
            for grp in range(2):
                j0, sl = grp * 4, grp
                for j in range(4):
                    kc = j0 + j
                    trans(ptrF32[0:128, sl * 512 + j * rows_n:
                                 sl * 512 + (j + 1) * rows_n],
                          shq[0:rows_n, kc * 128:(kc + 1) * 128],
                          ident[0:rows_n, 0:rows_n],
                          reads=("shq", "c.ident"), writes=(f"ptr.s{sl}",))
                cp_s(trst[:, sl * 512:sl * 512 + 4 * rows_n],
                     ptrF32[0:128, sl * 512:sl * 512 + 4 * rows_n],
                     (f"ptr.s{sl}",), (f"trst.s{sl}",))
                dma("scalar",
                    wgu_c[drow0 + j0 * 128:drow0 + (j0 + 4) * 128,
                          rt * 128:rt * 128 + rows_n]
                    .rearrange("(kc p) n -> p kc n", p=128),
                    trst[:, sl * 512:sl * 512 + 4 * rows_n]
                    .rearrange("p (kc n) -> p kc n", n=rows_n),
                    (f"trst.s{sl}",),
                    (f"wguc.{drow0}.{rt}.{j0}",))
    for half, (hc0, w) in enumerate(((0, 1024), (1024, 1024), (2048, 768))):
        dma("sync", w_st[:, 0:w], wd_sh_d[:, hc0:hc0 + w], (), ("w_st",))
        quant(w_st[:, 0:w], shq[:, 0:w], 128, w, ("w_st",), ("shq",))
        nj = w // 128
        for grp in range((nj + 3) // 4):
            j0 = grp * 4
            j1 = min(j0 + 4, nj)
            nb = j1 - j0
            sl = grp % 2
            for j in range(j0, j1):
                trans(ptrF32[0:128, sl * 512 + (j - j0) * 128:
                             sl * 512 + (j - j0 + 1) * 128],
                      shq[:, j * 128:(j + 1) * 128], ident[:],
                      reads=("shq", "c.ident"), writes=(f"ptr.s{sl}",))
            cp_s(trst[:, sl * 512:sl * 512 + nb * 128],
                 ptrF32[0:128, sl * 512:sl * 512 + nb * 128],
                 (f"ptr.s{sl}",), (f"trst.s{sl}",))
            kc0 = hc0 // 128 + j0
            dma("scalar",
                wd_c[kc0 * 128:(kc0 + nb) * 128, :]
                .rearrange("(kc p) n -> p kc n", p=128),
                trst[:, sl * 512:sl * 512 + nb * 128]
                .rearrange("p (kc n) -> p kc n", n=128),
                (f"trst.s{sl}",), (f"wdc.{kc0}",))
    shard_tok = P.fence("scalar", "d_scalar", "shardw")
    P.op("gpsimd", lambda e: e.collective_compute(
        "AllGather", ALU.bypass, replica_groups=RG,
        ins=[wgu_c.opt()], outs=[wgu_g.opt()]),
        reads=(shard_tok,), writes=("wgu_g",), sem="col_wgu", inc=1)
    P.op("gpsimd", lambda e: e.collective_compute(
        "AllGather", ALU.bypass, replica_groups=RG,
        ins=[wd_c.opt()], outs=[wd_g.opt()]),
        reads=(shard_tok,), writes=("wd_g",), sem="col_wd", inc=1)
    esSh.close()

    qkv_fm = sbB("qkv_fm", [128, 8 * 3072], BF16)
    rqw_fm = sbB("rqw_fm", [128, 8 * 256], BF16)
    routw_fm = sbB("routw_fm", [128, 2 * 1024], BF16)
    outw_fm = sbB("outw_fm", [128, 8 * 1024], BF16)
    esW = ExitStack()
    rkvw_fm = esW.enter_context(
        nc.sbuf_tensor("rkvw_fm", [128, 8 * 512], BF16))
    contrib = esW.enter_context(nc.sbuf_tensor("contrib", [8, 512], BF16))
    gath = esW.enter_context(nc.sbuf_tensor("gath", [64, 512], BF16))

    for dst, srcw, kcn, rname in (
            (qkv_fm, qkvw_d, 8, "w.qkv"), (rqw_fm, rqw_d, 8, "w.rq"),
            (rkvw_fm, rkvw_d, 8, "w.rkv"), (routw_fm, routw_d, 2, "w.rout"),
            (outw_fm, outw_d, 8, "w.out")):
        dma("gpsimd", dst[:].rearrange("p (kc n) -> p kc n", kc=kcn),
            srcw.rearrange("(kc p) n -> p kc n", p=128), (), (rname,),
            serial=False)
    wtok = P.fence("gpsimd", "d_gpsimd", "wts")
    P.rebind(("w.qkv", "w.rq", "w.rkv", "w.rout", "w.out"), wtok)

    def xcol(kc, hc):
        return kc * (NHC * 128) + hc * 128

    def c_stage1(hc):
        par = hc % 2
        if hc == 0:
            P.op("sync", lambda e, o=x_roll[:, 0:1024], i=x_d[0:128, :]:
                 e.dma_start(out=o, in_=i), reads=(), writes=("xroll.0",),
                 sem="d_x0", inc=16)
        if hc + 1 < NHC:
            npar = (hc + 1) % 2
            P.op("sync", lambda e,
                 o=x_roll[:, npar * 1024:(npar + 1) * 1024],
                 i=x_d[(hc + 1) * 128:(hc + 2) * 128, :]:
                 e.dma_start(out=o, in_=i), reads=(),
                 writes=(f"xroll.{npar}",), sem=f"d_x{npar}", inc=16)
        xr = x_roll[:, par * 1024:(par + 1) * 1024]
        rx = f"xroll.{par}"
        P.op("vector", lambda e, o=bnst[par][:, 0:6], i=xr[:, 0:512]:
             e.bn_stats(o, i), reads=(rx,), writes=(f"bn0.{par}",))
        P.op("vector", lambda e, o=bnst[par][:, 6:12], i=xr[:, 512:1024]:
             e.bn_stats(o, i), reads=(rx,), writes=(f"bn1.{par}",))
        P.op("vector", lambda e, o=mv[par][:], i=bnst[par][:]:
             e.bn_aggr(o, i),
             reads=(f"bn0.{par}", f"bn1.{par}"), writes=(f"mv.{par}",))
        act(rstd[par][:], mv[par][:, 1:2], AF.Ln,
            (f"mv.{par}", "c.cv0"), (f"rstd.{par}",), bias=cvec[:, 0:1])
        act(rstd[par][:], rstd[par][:], AF.Exp, (f"rstd.{par}",),
            (f"rstd.{par}",), scale=-0.5)
        P.op("vector", lambda e, o=xn_tm[par], i=xr, m=mv[par][:, 0:1],
             r=rstd[par][:]:
             e.tensor_scalar(o, i, m, r, ALU.subtract, ALU.mult),
             reads=(rx, f"mv.{par}", f"rstd.{par}"), writes=(rx,))
        cp_s(xn_bf[par][:], xn_tm[par], (rx,),
             (f"xn_bf{par}",))
        for grp in range(2):
            j0, sl = grp * 4, grp
            for j in range(4):
                kc = j0 + j
                trans(ptrF32[0:128, sl * 512 + j * 128:
                             sl * 512 + (j + 1) * 128],
                      xn_bf[par][:, kc * 128:(kc + 1) * 128], ident[:],
                      reads=(f"xn_bf{par}", "c.ident"),
                      writes=(f"ptr.s{sl}",))
            cp_s(xn_fm[:].rearrange("p (kc t) -> p kc t", kc=8)
                 [:, j0:j0 + 4, hc * 128:(hc + 1) * 128],
                 ptrF32[0:128, sl * 512:sl * 512 + 512]
                 .rearrange("p (kc t) -> p kc t", kc=4),
                 (f"ptr.s{sl}",),
                 tuple(f"xnfm.{kc}.{hc}" for kc in range(j0, j0 + 4)))

    def c_quant(hc):
        par = hc % 2
        t = hc - 1
        quant(xn_tm[par], xq_tm[par][:], 128, 1024,
              (f"xroll.{par}",), (f"xn_bf{par}",))
        if t % 2 == 1:
            pi = t // 2
            P.op("vector", lambda e,
                 o=p4[0:128, pi * 8:(pi + 1) * 8],
                 i=xn_fm[:].rearrange("p (kc c) -> p kc c", kc=8)
                 [:, :, t * 128:(t + 2) * 128]:
                 e.tensor_reduce(o, i, AX.X, ALU.add),
                 reads=tuple(f"xnfm.{kc}.{hc}" for kc in range(8)
                             for hc in (t, t + 1)),
                 writes=("p4",))

    def c_stage2(hc):
        par = hc % 2
        t = hc - 1
        for grp in range(2):
            j0, sl = grp * 4, grp
            for j in range(4):
                kc = j0 + j
                trans(ptrF32[0:128, sl * 512 + j * 128:
                             sl * 512 + (j + 1) * 128],
                      xq_tm[par][:, kc * 128:(kc + 1) * 128], ident[:],
                      reads=(f"xn_bf{par}", "c.ident"),
                      writes=(f"ptr.s{sl}",))
            cp_s(xqT_st[:, j0 * 128:(j0 + 4) * 128],
                 ptrF32[0:128, sl * 512:sl * 512 + 512],
                 (f"ptr.s{sl}",), ("xqT_st",))
        P.op("scalar", lambda e, o=xq_d[:, t * 1024:(t + 1) * 1024],
             i=xqT_st[:]: e.dma_start(out=o, in_=i),
             reads=("xqT_st",), writes=(f"xqd.{t}",), sem="d_xqst", inc=16)

    for i in range(NHC + 2):
        if 2 <= i <= NHC:
            c_quant(i - 1)
        if 3 <= i <= NHC + 1:
            c_stage2(i - 2)
        if i < NHC:
            c_stage1(i)

    act(pooled_bf[:].rearrange("p (kc pi) -> p kc pi", pi=8),
        p4[0:128, 0:64].rearrange("p (pi kc) -> p kc pi", kc=8),
        AF.Identity, ("p4",), ("pooled_bf",), scale=1.0 / 256.0)
    for mc in range(4):
        bank, bres = (psE, "E0") if mc % 2 == 0 else (psO, "O0")
        for kc in range(8):
            mm(bank[0:128, mc * 8:(mc + 1) * 8],
               rkvw_fm[:, kc * 512 + mc * 128:kc * 512 + (mc + 1) * 128],
               pooled_bf[:, kc * 8:(kc + 1) * 8],
               start=(kc == 0), stop=(kc == 7),
               reads=("pooled_bf", "w.rkv", wtok), writes=(bres,))
        P.op("scalar", lambda e, o=rkv_bf[:, mc * 8:(mc + 1) * 8],
             i=bank[0:128, mc * 8:(mc + 1) * 8], b=rkvb_fm[:, mc:mc + 1]:
             e.activation(o, i, AF.Identity, bias=b),
             reads=(bres, "c.rkvb"), writes=(f"rkvbf.{mc}",))
    for mc in range(4):
        trans(ptrF32[0:8, mc * 128:(mc + 1) * 128],
              rkv_bf[:, mc * 8:(mc + 1) * 8], ident[:],
              reads=(f"rkvbf.{mc}", "c.ident"), writes=("ptr.s0",))
    cp_s(contrib[:], ptrF32[0:8, 0:512], ("ptr.s0",),
         tuple(f"contrib.{mc}" for mc in range(4)))
    dma("gpsimd", rkv_c, contrib[:],
        tuple(f"contrib.{mc}" for mc in range(4)), ("rkv_c",))
    P.op("gpsimd", lambda e: e.collective_compute(
        "AllGather", ALU.bypass, replica_groups=RG,
        ins=[rkv_c.opt()], outs=[rkv_g.opt()]),
        reads=("rkv_c",), writes=("rkv_g",), sem="col_rkv", inc=1)

    esW.close()

    k_roll = sbB("k_roll", [128, 8 * 3 * 128], BF16)
    v_roll = sbB("v_roll", [128, 3 * 16 * 65], BF16)
    q_fm = sbB("q_fm", [128, 8 * 256], BF16)
    rq_fm = sbB("rq_fm", [128, 4 * 512], BF16)
    e_raw = [sbB(f"e_raw{i}", [128, 768], BF16) for i in range(4)]
    u_bf = sbB("u_bf", [128, 2 * 256], BF16)
    Dt = [sbB(f"Dt{i}", [128, 256], F32) for i in range(2)]
    rdt0 = sbB("rdt0", [128, 256], F32)
    z10 = sbB("z10", [128, 256], F32)
    w10 = sbB("w10", [128, 256], F32)
    rdt = [rdt0, rdt0]
    z1 = [z10, z10]
    w1 = [w10, w10]
    a4 = [sbB(f"a4{i}", [128, 256], BF16) for i in range(2)]
    b4 = [sbB(f"b4{i}", [128, 256], BF16) for i in range(2)]
    ones2 = sbB("ones2", [128, 64], BF16)
    lat_row = sbB("lat_row", [1, 512], F32)
    lat_rbf = sbB("lat_rbf", [1, 512], BF16)
    a_bf = sbB("a_bf", [128, 256], BF16)
    b_bf = sbB("b_bf", [128, 256], BF16)
    latout_fm = sbB("latout_fm", [128, 2 * 2048], BF16)
    remote_fm = sbB("remote_fm", [128, 8 * 256], BF16)
    mixed_fm = sbB("mixed_fm", [128, 8 * 256], BF16)

    memset(ones2[:], 1.0, ("c.ones2",))
    for i in range(2):
        memset(Dt[i][:], 0.0, (f"Dt{i}",))
        P.op("vector", lambda e, o=ptrF32[:, i * 512:(i + 1) * 512]:
             e.memset(o, 0.0), reads=(), writes=(f"ptr.s{i}",))

    def kv_bank(i):
        return (psE, "E0") if i % 2 == 0 else (psO, "O0")

    def produce_k_one(hc):
        slot = hc % 3
        xn_r = tuple(f"xnfm.{kc}.{hc}" for kc in range(8))
        for p in range(8):
            bank, bres = kv_bank(p)
            for kc in range(8):
                mm(bank[0:128, 0:128],
                   qkv_fm[:, kc * 3072 + 1024 + p * 128:
                          kc * 3072 + 1024 + (p + 1) * 128],
                   xn_fm[:, xcol(kc, hc):xcol(kc, hc) + 128],
                   start=(kc == 0), stop=(kc == 7),
                   reads=xn_r + ("w.qkv",), writes=(bres,))
            P.op("vector", lambda e,
                 o=k_roll[:, (p * 3 + slot) * 128:(p * 3 + slot) * 128 + 128],
                 i=bank[0:128, 0:128], b=kb_fm[:, p:p + 1]:
                 e.tensor_scalar(o, i, b, None, ALU.add),
                 reads=(bres, "c.kb"), writes=(f"kroll.{p}.{slot}",))

    def produce_k_pair(hc1):
        s1, s2 = hc1 % 3, (hc1 + 1) % 3
        xn_r = tuple(f"xnfm.{kc}.{hc}" for kc in range(8)
                     for hc in (hc1, hc1 + 1))
        for p in range(8):
            bank, bres = kv_bank(p)
            for kc in range(8):
                mm(bank[0:128, 0:256],
                   qkv_fm[:, kc * 3072 + 1024 + p * 128:
                          kc * 3072 + 1024 + (p + 1) * 128],
                   xn_fm[:, xcol(kc, hc1):xcol(kc, hc1) + 256],
                   start=(kc == 0), stop=(kc == 7),
                   reads=xn_r + ("w.qkv",), writes=(bres,))
            for ci, slot in ((0, s1), (1, s2)):
                P.op("vector", lambda e,
                     o=k_roll[:, (p * 3 + slot) * 128:
                              (p * 3 + slot) * 128 + 128],
                     i=bank[0:128, ci * 128:(ci + 1) * 128],
                     b=kb_fm[:, p:p + 1]:
                     e.tensor_scalar(o, i, b, None, ALU.add),
                     reads=(bres, "c.kb"), writes=(f"kroll.{p}.{slot}",))

    def produce_v(hc):
        slot = hc % 3
        xn_r = tuple(f"xnfm.{kc}.{hc}" for kc in range(8))
        for vt in range(2):
            bank, bres = kv_bank(vt)
            for kc in range(8):
                mm(bank[:, 0:512],
                   xn_fm[:, xcol(kc, hc):xcol(kc, hc) + 128],
                   qkv_fm[:, kc * 3072 + 2048 + vt * 512:
                          kc * 3072 + 2048 + (vt + 1) * 512],
                   start=(kc == 0), stop=False,
                   reads=xn_r + ("w.qkv",), writes=(bres,))
            mm(bank[:, 0:512], ones_r[0:1, 0:128],
               vb_row[0:1, vt * 512:(vt + 1) * 512],
               start=False, stop=True, reads=("c.ones", "c.vb"),
               writes=(bres,))
            P.op("vector", lambda e,
                 o=v_roll[:, (slot * 16 + vt * 8) * 65:
                          (slot * 16 + vt * 8 + 8) * 65].rearrange(
                     "p (h c) -> p h c", c=65)[:, :, 0:64],
                 i=bank[:, 0:512].rearrange("p (h c) -> p h c", c=64):
                 e.tensor_copy(o, i),
                 reads=(bres,), writes=(f"vroll.{vt}.{slot}",))

    def qblock(b):
        hcs = (2 * b, 2 * b + 1, 2 * b + 2)
        if b == 0:
            produce_k_one(0)
            produce_v(0)
        produce_k_pair(2 * b + 1)
        produce_v(2 * b + 1)
        produce_v(2 * b + 2)
        kv_r = tuple(f"kroll.{p}.{hc % 3}" for p in range(8) for hc in hcs) + \
            tuple(f"vroll.{vt}.{hc % 3}" for vt in range(2) for hc in hcs)
        for p in range(8):
            bank, bres = kv_bank(p)
            for kc in range(8):
                mm(bank[0:128, 0:256],
                   qkv_fm[:, kc * 3072 + p * 128:kc * 3072 + (p + 1) * 128],
                   xn_fm[:, xcol(kc, 2 * b + 1):xcol(kc, 2 * b + 1) + 256],
                   start=(kc == 0), stop=(kc == 7),
                   reads=(f"xnfm.{kc}.{2 * b + 1}", f"xnfm.{kc}.{2 * b + 2}",
                          "w.qkv"), writes=(bres,))
            P.op("vector", lambda e, o=q_fm[:, p * 256:(p + 1) * 256],
                 i=bank[0:128, 0:256], b=qb_fm[:, p:p + 1]:
                 e.tensor_scalar(o, i, b, None, ALU.add),
                 reads=(bres, "c.qb"), writes=(f"qfm.{p}",))
        lat_r = tuple(f"latout.{lh}.{b // 2}" for lh in range(LH))
        for mc in range(8):
            bank, bres = kv_bank(mc)
            for kc in range(2):
                mm(bank[0:128, 0:256],
                   routw_fm[:, kc * 1024 + mc * 128:kc * 1024 + (mc + 1) * 128],
                   latout_fm[:, kc * 2048 + b * 256:
                             kc * 2048 + (b + 1) * 256],
                   start=(kc == 0), stop=(kc == 1),
                   reads=lat_r + ("w.rout",), writes=(bres,))
            P.op("vector", lambda e, o=remote_fm[:, mc * 256:(mc + 1) * 256],
                 i=bank[0:128, 0:256], b=routb_fm[:, mc:mc + 1]:
                 e.tensor_scalar(o, i, b, None, ALU.add),
                 reads=(bres, "c.routb"), writes=(f"remote.{mc}",))
        mres = "c.mask0" if b == 0 else "c.maskg"
        moff = 0 if b == 0 else 768

        def head_scores(h):
            par = h % 2
            p = h // 2
            ph = par * 64
            psH = psE if par == 0 else psO
            bn0 = "E0" if par == 0 else "O0"
            bn1 = "E1" if par == 0 else "O1"
            for c3 in range(3):
                slot = hcs[c3] % 3
                dst = (psH[0:128, c3 * 256:(c3 + 1) * 256] if c3 < 2
                       else psH[0:128, 512:768])
                mm(dst,
                   k_roll[ph:ph + 64,
                          (p * 3 + slot) * 128:(p * 3 + slot) * 128 + 128],
                   q_fm[ph:ph + 64, p * 256:(p + 1) * 256],
                   start=True, stop=True,
                   reads=kv_r + (f"qfm.{p}",),
                   writes=(bn0 if c3 < 2 else bn1,))

        def head_expmask(h):
            par = h % 2
            e4 = h % 4
            psH = psE if par == 0 else psO
            bn0 = "E0" if par == 0 else "O0"
            bn1 = "E1" if par == 0 else "O1"
            er = e_raw[e4]
            act(er[:], psH[0:128, 0:768], AF.Exp, (bn0, bn1),
                (f"eraw{e4}",), scale=0.125)
            tt(er[:], er[:], masks[:, moff:moff + 768], ALU.mult,
               (f"eraw{e4}", mres), (f"eraw{e4}",))

        def head_U(h):
            par = h % 2
            p = h // 2
            ph = par * 64
            g = h // 4
            j = h % 4
            gp = g % 2
            pr = (h % 4) // 2
            psH = psE if par == 0 else psO
            bn1 = "E1" if par == 0 else "O1"
            pU = p4 if par == 0 else p5
            bnU = "p4" if par == 0 else "p5"
            er = e_raw[h % 4]
            ptrF = ptrF32[:, gp * 512:(gp + 1) * 512]
            for c3 in range(3):
                slot = hcs[c3] % 3
                mm(pU[0:65, 0:256],
                   v_roll[:, (slot * 16 + h) * 65:(slot * 16 + h) * 65 + 65],
                   er[:, c3 * 256:(c3 + 1) * 256],
                   start=(c3 == 0), stop=(c3 == 2),
                   reads=(f"eraw{h % 4}",) + kv_r + ("vroll.ones",),
                   writes=(bnU,))
            cp_s(u_bf[ph:ph + 64, pr * 256:(pr + 1) * 256],
                 pU[0:64, 0:256], (bnU,), (f"ubf.{par}.{pr}",))
            act(Dt[gp][32 * j:32 * j + 1, 0:256], pU[64:65, 0:256], AF.Ln,
                (bnU, "c.cv1"), (f"Dt{gp}",), bias=cvec[0:1, 1:2])
            mm(ptrF[32 * j:32 * j + 1, 0:256], gl_col[ph:ph + 64, 0:1],
               u_bf[ph:ph + 64, pr * 256:(pr + 1) * 256],
               start=True, stop=True,
               reads=(f"ubf.{par}.{pr}", "c.gl"),
               writes=(f"ptr.s{gp}",), tp=(ph, 32 * j))
            mm(ptrF[32 * j:32 * j + 1, 256:512], gr2[:, par:par + 1],
               remote_fm[:, p * 256:(p + 1) * 256],
               start=True, stop=True, reads=(f"remote.{p}", "c.gr"),
               writes=(f"ptr.s{gp}",), tp=(0, 32 * j))

        def group_chain(g):
            gp = g % 2
            ptrF = ptrF32[:, gp * 512:(gp + 1) * 512]
            pres = f"ptr.s{gp}"
            dres = f"Dt{gp}"
            act(rdt[gp][:], Dt[gp][:], AF.Exp, (dres,), ("rdt",),
                scale=-1.0)
            P.op("vector", lambda e, o=z1[gp][:], i=rdt[gp][:],
                 r=ptrF[:, 0:256]: e.tensor_tensor(o, i, r, ALU.mult),
                 reads=("rdt", pres), writes=("z1",))
            P.op("vector", lambda e, o=z1[gp][:], i=z1[gp][:],
                 r=ptrF[:, 256:512]: e.tensor_tensor(o, i, r, ALU.add),
                 reads=("z1", pres), writes=("z1",))
            act(w1[gp][:], z1[gp][:], AF.Exp, ("z1",), ("w1",),
                scale=-1.0)
            act(rdt[gp][:], w1[gp][:], AF.Ln, ("w1", "c.cv2"),
                ("rdt",), bias=cvec[:, 2:3])
            tt(w1[gp][:], Dt[gp][:], rdt[gp][:], ALU.add,
               (dres, "rdt"), ("w1",))
            act(a4[gp][:], w1[gp][:], AF.Exp, ("w1",), (f"a4{gp}",),
                scale=-1.0)
            tt(w1[gp][:], z1[gp][:], rdt[gp][:], ALU.add,
               ("z1", "rdt"), ("w1",))
            act(b4[gp][:], w1[gp][:], AF.Exp, ("w1",), (f"b4{gp}",),
                scale=-1.0)

        def pair_mix(g, pr):
            gp = g % 2
            p = 2 * g + pr
            jA = 2 * pr
            jB = 2 * pr + 1
            psH = psE if pr == 0 else psO
            bn0 = "E0" if pr == 0 else "O0"
            bn1 = "E1" if pr == 0 else "O1"
            mm(psH[0:64, 768:1024], ones2[32 * jA:32 * jA + 1, 0:64],
               a4[gp][32 * jA:32 * jA + 1, 0:256],
               start=True, stop=True, reads=(f"a4{gp}", "c.ones2"),
               writes=(bn1,), tp=(32 * jA, 0))
            mm(psH[64:128, 768:1024], ones2[32 * jB:32 * jB + 1, 0:64],
               a4[gp][32 * jB:32 * jB + 1, 0:256],
               start=True, stop=True, reads=(f"a4{gp}", "c.ones2"),
               writes=(bn1,), tp=(32 * jB, 64))
            pz, bz = (p4, "p4") if pr == 0 else (p5, "p5")
            mm(pz[0:64, 256:512], ones2[32 * jA:32 * jA + 1, 0:64],
               b4[gp][32 * jA:32 * jA + 1, 0:256],
               start=True, stop=True, reads=(f"b4{gp}", "c.ones2"),
               writes=(bz,), tp=(32 * jA, 0))
            mm(pz[64:128, 256:512], ones2[32 * jB:32 * jB + 1, 0:64],
               b4[gp][32 * jB:32 * jB + 1, 0:256],
               start=True, stop=True, reads=(f"b4{gp}", "c.ones2"),
               writes=(bz,), tp=(32 * jB, 64))
            cp_s(a_bf[:], psH[:, 768:1024], (bn1,), ("abf",))
            cp_s(b_bf[:], pz[:, 256:512], (bz,), ("bbf",))
            P.op("vector", lambda e,
                 o=mixed_fm[:, p * 256:(p + 1) * 256],
                 i=u_bf[:, pr * 256:(pr + 1) * 256], r=a_bf[:]:
                 e.tensor_tensor(o, i, r, ALU.mult),
                 reads=(f"ubf.0.{pr}", f"ubf.1.{pr}", "abf"),
                 writes=(f"mixed.{p}",))
            P.op("vector", lambda e, o=b_bf[:],
                 i=remote_fm[:, p * 256:(p + 1) * 256], r=b_bf[:]:
                 e.tensor_tensor(o, i, r, ALU.mult),
                 reads=(f"remote.{p}", "bbf"), writes=("bbf",))
            P.op("vector", lambda e,
                 o=mixed_fm[:, p * 256:(p + 1) * 256],
                 i=mixed_fm[:, p * 256:(p + 1) * 256], r=b_bf[:]:
                 e.tensor_tensor(o, i, r, ALU.add),
                 reads=(f"mixed.{p}", "bbf"), writes=(f"mixed.{p}",))

        for pi_ in range(10):
            if pi_ < 8:
                head_scores(2 * pi_)
                head_scores(2 * pi_ + 1)
            if pi_ >= 3 and pi_ % 2 == 1:
                g = (pi_ - 3) // 2
                pair_mix(g, 0)
                pair_mix(g, 1)
            if pi_ < 8:
                head_expmask(2 * pi_)
                head_expmask(2 * pi_ + 1)
            if 1 <= pi_ <= 8:
                for h in (2 * pi_ - 2, 2 * pi_ - 1):
                    head_U(h)
                    if h % 4 == 3:
                        group_chain(h // 4)
        mix_r = tuple(f"mixed.{p}" for p in range(8))
        for half in range(2):
            t = 2 * b + half
            dma("sync", x_res[:], x_d[(t + 1) * 128:(t + 2) * 128, :], (),
                ("x_res",) + (("lat_u",) if b == 0 and half == 0 else ()))
            for nt in range(2):
                bank, bres = kv_bank(nt)
                for kc in range(8):
                    mm(bank[:, 0:512],
                       mixed_fm[:, kc * 256 + half * 128:
                                kc * 256 + half * 128 + 128],
                       outw_fm[:, kc * 1024 + nt * 512:
                               kc * 1024 + (nt + 1) * 512],
                       start=(kc == 0), stop=False,
                       reads=mix_r + ("w.out",), writes=(bres,))
                mm(bank[:, 0:512], ones_r[0:1, 0:128],
                   outb_row[0:1, nt * 512:(nt + 1) * 512],
                   start=False, stop=True, reads=("c.ones", "c.outb"),
                   writes=(bres,))
                P.op("vector", lambda e, o=x_res[:, nt * 512:(nt + 1) * 512],
                     i=bank[:, 0:512], r=x_res[:, nt * 512:(nt + 1) * 512]:
                     e.tensor_tensor(o, i, r, ALU.add),
                     reads=(bres, "x_res"), writes=("x_res",))
            P.op("scalar", lambda e, o=y_d[t * 128:(t + 1) * 128, :],
                 i=x_res[:]: e.dma_start(out=o, in_=i),
                 reads=("x_res",), writes=(f"y1d.{t}",), sem="d_y1st",
                 inc=16)

    lat_u = x_res[0:64, 512:1024]
    def rq_emit(nc4):
        sl4 = nc4 % 2
        for mc in range(2):
            bank, bres = kv_bank(mc)
            for kc in range(8):
                mm(bank[0:128, 0:512],
                   rqw_fm[:, kc * 256 + mc * 128:kc * 256 + (mc + 1) * 128],
                   xn_fm[:, xcol(kc, 4 * nc4 + 1):xcol(kc, 4 * nc4 + 1) + 512],
                   start=(kc == 0), stop=(kc == 7),
                   reads=tuple(f"xnfm.{kc}.{hc}"
                               for hc in range(4 * nc4 + 1, 4 * nc4 + 5))
                   + ("w.rq",), writes=(bres,))
            P.op("vector", lambda e,
                 o=rq_fm[:, (sl4 * 2 + mc) * 512:(sl4 * 2 + mc + 1) * 512],
                 i=bank[0:128, 0:512], b=rqb_fm[:, mc:mc + 1]:
                 e.tensor_scalar(o, i, b, None, ALU.add),
                 reads=(bres, "c.rqb"), writes=(f"rqfm.{sl4}.{mc}",))

    rq_emit(0)
    rq_emit(1)
    dma("gpsimd", gath[:], rkv_g, ("rkv_g",), ("gath",))
    for lh in range(LH):
        trans(ptrF32[0:64, lh * 64:(lh + 1) * 64],
              gath[:, lh * 64:lh * 64 + 64],
              ident[0:64, 0:64],
              reads=("gath", "c.ident"), writes=("ptr.s0",))
    for lh in range(LH):
        cp_s(rk_fm[(lh % 2) * 64:(lh % 2) * 64 + 64,
                   (lh // 2) * 64:(lh // 2) * 64 + 64],
             ptrF32[0:64, lh * 64:(lh + 1) * 64],
             ("ptr.s0",), (f"rkfm.{lh}",))
        cp_v(rv_l[:, lh * 65:lh * 65 + 64],
             gath[:, 256 + lh * 64:256 + lh * 64 + 64], ("gath",),
             (f"rvl.{lh}",))
    memset(rv_l[:, 64::65], 1.0, ("rvl.ones",))
    memset(v_roll[:, 64::65], 1.0,
           ("vroll.ones", "gath") + tuple(f"contrib.{mc}" for mc in range(4)))

    for nc4 in range(4):
        for lh in range(LH):
            it = nc4 * LH + lh
            pb = (lh % 2) * 64
            sbank, sres = (psE, "E0") if it % 2 == 0 else (psO, "O0")
            pU, bnU = (p4, "p4") if it % 2 == 0 else (p5, "p5")
            pB, bnB = (p5, "p5") if it % 2 == 0 else (p4, "p4")
            le = e_raw[it % 2][0:64, 0:512]
            ler = f"eraw{it % 2}"
            mm(sbank[0:64, 0:512],
               rk_fm[pb:pb + 64, (lh // 2) * 64:(lh // 2) * 64 + 64],
               rq_fm[pb:pb + 64,
                     ((nc4 % 2) * 2 + lh // 2) * 512:
                     ((nc4 % 2) * 2 + lh // 2 + 1) * 512],
               start=True, stop=True,
               reads=(f"rkfm.{lh}", f"rqfm.{nc4 % 2}.{lh // 2}"),
               writes=(sres,))
            act(le, sbank[0:64, 0:512], AF.Exp, (sres,), (ler,), scale=0.125)
            tt(le, le, masklat[:, nc4 * 512:(nc4 + 1) * 512], ALU.mult,
               (ler, "c.masklat"), (ler,))
            mm(pU[0:65, 0:512], rv_l[:, lh * 65:lh * 65 + 65], le,
               start=True, stop=True,
               reads=(ler, f"rvl.{lh}", "rvl.ones"), writes=(bnU,))
            act(lat_row[0:1, 0:512], pU[64:65, 0:512], AF.Ln,
                (bnU, "c.cv1"), ("lrow.0",), bias=cvec[0:1, 1:2])
            act(lat_row[0:1, 0:512], lat_row[0:1, 0:512], AF.Exp,
                ("lrow.0",), ("lrow.0",), scale=-1.0)
            cp_v(lat_rbf[0:1, 0:512], lat_row[0:1, 0:512], ("lrow.0",),
                 ("lrbf",))
            mm(pB[0:64, 0:512], ones_r[0:1, 0:64], lat_rbf[0:1, 0:512],
               start=True, stop=True, reads=("lrbf", "c.ones"),
               writes=(bnB,))
            cp_s(lat_u[:, 0:512], pU[0:64, 0:512], (bnU,), ("lat_u",))
            P.op("vector", lambda e,
                 o=latout_fm[pb:pb + 64,
                             (lh // 2) * 2048 + nc4 * 512:
                             (lh // 2) * 2048 + (nc4 + 1) * 512],
                 i=lat_u[:, 0:512], r=pB[0:64, 0:512]:
                 e.tensor_tensor(o, i, r, ALU.mult),
                 reads=("lat_u", bnB), writes=(f"latout.{lh}.{nc4}",))
        if nc4 + 2 < 4:
            rq_emit(nc4 + 2)

    for b in range(NQB):
        qblock(b)

    P.barrier()
    esB.close()

    esA = ExitStack()

    def sbA(name, shape, dt, **kw):
        return esA.enter_context(nc.sbuf_tensor(name, list(shape), dt, **kw))

    wgT = sbA("wgT", [128, 8 * FFI], BF16)
    wuT = sbA("wuT", [128, 8 * FFI], BF16)
    wdT = sbA("wdT", [128, 22 * 1024], BF16)
    xq_t = [sbA(f"xq_t{i}", [128, 1024], BF16) for i in range(2)]
    h_tm = [sbA(f"h_tm{i}", [128, 1024], F32) for i in range(4)]
    h_q = [sbA(f"h_q{i}", [128, 1024], BF16) for i in range(2)]
    hqT = [sbA(f"hqT{i}", [128, 1024], BF16) for i in range(2)]
    y1_in = [sbA(f"y1_in{i}", [128, 1024], F32) for i in range(2)]
    y_out = [sbA(f"y_out{i}", [128, 1024], F32) for i in range(2)]

    for t0 in range(2):
        P.op("sync", lambda e, o=y1_in[t0][:],
             i=y_d[t0 * 128:(t0 + 1) * 128, :]: e.dma_start(out=o, in_=i),
             reads=(f"y1d.{t0}",), writes=(f"y1_in{t0}",),
             sem=f"d_ya{t0}", inc=16)
        P.op("sync", lambda e, o=xq_t[t0][:],
             i=xq_d[:, t0 * 1024:(t0 + 1) * 1024]: e.dma_start(out=o, in_=i),
             reads=(f"xqd.{t0}",), writes=(f"xq_t{t0}",),
             sem=f"d_xa{t0}", inc=16)
    for r in range(NCORES):
        g = r // 3
        P.op("sync", lambda e,
             o=wgT[:].rearrange("p (kc n) -> p kc n", kc=8)
             [:, :, r * 352:(r + 1) * 352],
             i=wgu_g[r * 2048:r * 2048 + 1024, :].rearrange(
                 "(kc p) n -> p kc n", p=128):
             e.dma_start(out=o, in_=i),
             reads=("wgu_g",), writes=(f"wgT.{r}",),
             sem=f"d_wg{g}", inc=16, serial=False)
        P.op("gpsimd", lambda e,
             o=wuT[:].rearrange("p (kc n) -> p kc n", kc=8)
             [:, :, r * 352:(r + 1) * 352],
             i=wgu_g[r * 2048 + 1024:r * 2048 + 2048, :].rearrange(
                 "(kc p) n -> p kc n", p=128):
             e.dma_start(out=o, in_=i),
             reads=("wgu_g",), writes=(f"wuT.{r}",),
             sem=f"d_wu{g}", inc=16, serial=False)
        P.op("scalar", lambda e,
             o=wdT[:].rearrange("p (kc n) -> p kc n", kc=22)
             [:, :, r * 128:(r + 1) * 128],
             i=wd_g[r * FFI:(r + 1) * FFI, :].rearrange(
                 "(kc p) n -> p kc n", p=128):
             e.dma_start(out=o, in_=i),
             reads=("wd_g",), writes=(f"wdT.{r}",),
             sem="d_wd", inc=16, serial=False)

    GLAST = (2, 5, 7)

    def rtok(pfx, n0, nw):
        g = ((n0 + nw - 1) // 352) // 3
        return tuple(f"{pfx}.{GLAST[gi]}" for gi in range(g + 1))

    SLICES = ((0, 1024), (1024, 1024), (2048, 768))
    slices = [(t, si, c0, w) for t in range(NCH)
              for si, (c0, w) in enumerate(SLICES)]

    def a_stage1a(i):
        t, si, c0, w = slices[i]
        tp_ = t % 2
        if si == 0 and t >= 2:
            P.op("sync", lambda e, o=y1_in[tp_][:],
                 i=y_d[t * 128:(t + 1) * 128, :]: e.dma_start(out=o, in_=i),
                 reads=(f"y1d.{t}",), writes=(f"y1_in{tp_}",),
                 sem=f"d_ya{tp_}", inc=16)
            P.op("sync", lambda e, o=xq_t[tp_][:],
                 i=xq_d[:, t * 1024:(t + 1) * 1024]: e.dma_start(out=o, in_=i),
                 reads=(f"xqd.{t}",), writes=(f"xq_t{tp_}",),
                 sem=f"d_xa{tp_}", inc=16)
        hp = i % 4
        for wT, pb, res0, res1, rr in ((wgT, psE, "E0", "E1", "wgT"),
                                       (wuT, psO, "O0", "O1", "wuT")):
            for kc in range(8):
                mm(pb[0:128, 0:512],
                   xq_t[tp_][:, kc * 128:(kc + 1) * 128],
                   wT[:, kc * FFI + c0:kc * FFI + c0 + 512],
                   start=(kc == 0), stop=(kc == 7),
                   reads=rtok(rr, c0, 512) + (f"xq_t{tp_}",),
                   writes=(res0,))
                mm(pb[0:128, 512:w],
                   xq_t[tp_][:, kc * 128:(kc + 1) * 128],
                   wT[:, kc * FFI + c0 + 512:kc * FFI + c0 + w],
                   start=(kc == 0), stop=(kc == 7),
                   reads=rtok(rr, c0, w) + (f"xq_t{tp_}",),
                   writes=(res1,))
        act(h_tm[hp][:, 0:w], psE[0:128, 0:w], AF.Silu, ("E0", "E1"),
            (f"hs{hp}",))

    def a_stage1b(i):
        t, si, c0, w = slices[i]
        hp = i % 4
        P.op("vector", lambda e, o=h_tm[hp][:, 0:w],
             i=h_tm[hp][:, 0:w], r=psO[0:128, 0:w]:
             e.tensor_tensor(o, i, r, ALU.mult),
             reads=(f"hs{hp}", "O0", "O1"),
             writes=(f"h{hp}.0", f"h{hp}.512"))

    def a_quant(i, half):
        t, si, c0, w = slices[i]
        par = i % 2
        hp = i % 4
        h0 = half * 512
        hw = 512 if half == 0 else w - 512
        quant(h_tm[hp][:, h0:h0 + hw], h_q[par][:, h0:h0 + hw], 128, hw,
              (f"h{hp}.{h0}",), (f"hq{par}",))

    def a_stage2(i):
        t, si, c0, w = slices[i]
        tp_ = t % 2
        par = i % 2
        nkc = w // 128
        for grp in range((nkc + 3) // 4):
            j0 = grp * 4
            j1 = min(j0 + 4, nkc)
            sl = grp % 2
            for j in range(j0, j1):
                trans(ptrF32[0:128, sl * 512 + (j - j0) * 128:
                             sl * 512 + (j - j0 + 1) * 128],
                      h_q[par][:, j * 128:(j + 1) * 128], ident[:],
                      reads=(f"hq{par}", "c.ident"), writes=(f"ptr.s{sl}",))
            cp_s(hqT[par][:, j0 * 128:j1 * 128],
                 ptrF32[0:128, sl * 512:sl * 512 + (j1 - j0) * 128],
                 (f"ptr.s{sl}",), (f"hqT{par}",))
        for j in range(nkc):
            kc = c0 // 128 + j
            for nt in range(2):
                bank, bres = (p4, "p4") if nt == 0 else (p5, "p5")
                mm(bank[:, 0:512],
                   hqT[par][:, j * 128:(j + 1) * 128],
                   wdT[:, kc * 1024 + nt * 512:kc * 1024 + (nt + 1) * 512],
                   start=(kc == 0), stop=(kc == 21),
                   reads=("wdT.7", f"hqT{par}"), writes=(bres,))
        if si == 2:
            for nt in range(2):
                bank, bres = (p4, "p4") if nt == 0 else (p5, "p5")
                P.op("vector", lambda e,
                     o=y_out[tp_][:, nt * 512:(nt + 1) * 512],
                     i=bank[:, 0:512],
                     r=y1_in[tp_][:, nt * 512:(nt + 1) * 512]:
                     e.tensor_tensor(o, i, r, ALU.add),
                     reads=(bres, f"y1_in{tp_}"), writes=(f"yo{tp_}.{nt}",))
            P.op("scalar", lambda e, o=y_d[t * 128:(t + 1) * 128, :],
                 i=y_out[tp_][:]: e.dma_start(out=o, in_=i),
                 reads=(f"yo{tp_}.0", f"yo{tp_}.1"), writes=(f"yfinal.{t}",),
                 sem=f"d_yf{tp_}", inc=16)

    for i in range(len(slices) + 2):
        if 1 <= i <= len(slices):
            a_quant(i - 1, 0)
        if i < len(slices):
            a_stage1a(i)
        if 1 <= i <= len(slices):
            a_quant(i - 1, 1)
        if i < len(slices):
            a_stage1b(i)
        if i >= 2:
            a_stage2(i - 2)

    P.barrier()
    esA.close()

    sems = {}
    sem_stack = ExitStack()
    for name in sorted(P.sems):
        sems[name] = sem_stack.enter_context(nc.semaphore(name))
    with nc.Block() as block:
        P.emit_all(block, sems)
    sem_stack.close()
    es.close()
    return nc


_NC_CACHE = {}


def kernel(L0_attn_x, L0_attn_qkv_w, L0_attn_qkv_b, L0_attn_rq_w, L0_attn_rq_b,
           L0_attn_rkv_w, L0_attn_rkv_b, L0_attn_rout_w, L0_attn_rout_b,
           L0_attn_out_w, L0_attn_out_b, L0_attn_gl_w, L0_attn_gr_w,
           L0_ffn_w_gate, L0_ffn_w_up, L0_ffn_w_down):
    f32 = np.float32
    x = np.ascontiguousarray(np.asarray(L0_attn_x, f32).reshape(S, D))
    qkv_w = np.ascontiguousarray(np.asarray(L0_attn_qkv_w, f32))
    qkv_b = np.asarray(L0_attn_qkv_b, f32)
    rq_w = np.ascontiguousarray(np.asarray(L0_attn_rq_w, f32))
    rq_b = np.asarray(L0_attn_rq_b, f32)
    rkv_w = np.ascontiguousarray(np.asarray(L0_attn_rkv_w, f32))
    rkv_b = np.asarray(L0_attn_rkv_b, f32)
    rout_w = np.ascontiguousarray(np.asarray(L0_attn_rout_w, f32))
    rout_b = np.asarray(L0_attn_rout_b, f32)
    out_w = np.ascontiguousarray(np.asarray(L0_attn_out_w, f32))
    out_b = np.asarray(L0_attn_out_b, f32)
    gl_w = np.asarray(L0_attn_gl_w, f32)
    gr_w = np.asarray(L0_attn_gr_w, f32)
    w_gate = np.ascontiguousarray(np.asarray(L0_ffn_w_gate, f32))
    w_up = np.ascontiguousarray(np.asarray(L0_ffn_w_up, f32))
    w_down = np.ascontiguousarray(np.asarray(L0_ffn_w_down, f32))

    def fm(vec, cols):
        return np.ascontiguousarray(vec.reshape(cols, 128).T)

    gl_col = np.ascontiguousarray(np.concatenate([gl_w[:, 0], gl_w[:, 0]]).reshape(128, 1))
    gr2 = np.zeros((128, 2), f32)
    gr2[0:64, 0] = gr_w[:, 0]
    gr2[64:128, 1] = gr_w[:, 0]
    ident = np.eye(128, dtype=BFnp)
    maskg = make_local_mask(False)
    mask0c0 = make_local_mask(True)

    nc = _NC_CACHE.get("nc")
    if nc is None:
        nc = build_program()
        _NC_CACHE["nc"] = nc
    in_maps = []
    for c in range(NCORES):
        lo = c * T - 128
        xh = np.zeros((NHC * 128, D), f32)
        if lo < 0:
            xh[128:] = x[0:T]
        else:
            xh[:] = x[lo:(c + 1) * T]
        in_maps.append({
            "x": xh,
            "qkv_w": qkv_w, "rq_w": rq_w, "rkv_w": rkv_w,
            "rout_w": rout_w, "out_w": out_w,
            "gl_col": gl_col, "gr2": gr2,
            "qb_fm": fm(qkv_b[0:1024], 8), "kb_fm": fm(qkv_b[1024:2048], 8),
            "vb_row": np.ascontiguousarray(qkv_b[2048:3072].reshape(1, D)),
            "rqb_fm": fm(rq_b, 2), "rkvb_fm": fm(rkv_b, 4),
            "routb_fm": fm(rout_b, 8),
            "outb_row": np.ascontiguousarray(out_b.reshape(1, D)),
            "wg_sh": np.ascontiguousarray(w_gate[c * 352:(c + 1) * 352]),
            "wu_sh": np.ascontiguousarray(w_up[c * 352:(c + 1) * 352]),
            "wd_sh": np.ascontiguousarray(w_down[c * 128:(c + 1) * 128]),
            "mask0": mask0c0 if c == 0 else maskg,
            "maskg": maskg,
            "mask_lat": make_lat_mask(c),
            "ident": ident,
        })
    res = run_bass_kernel_spmd(nc, in_maps, core_ids=list(range(NCORES)))
    y = np.concatenate([np.asarray(r["y"], np.float32)
                        for r in res.results], axis=0)
    return y.reshape(1, S, D)


# version 37
# speedup vs baseline: 1.0231x; 1.0231x over previous
import numpy as np
import ml_dtypes
import concourse.bass as bass
import concourse.mybir as mybir
import concourse.bass_utils as _bass_utils
from concourse.bass_utils import run_bass_kernel_spmd

_orig_run_command = _bass_utils.run_command


def _run_command_ldwopt(cmd, *a, **kw):
    if isinstance(cmd, list):
        cmd = ["--enable-ldw-opt=true" if c == "--enable-ldw-opt=false" else c
               for c in cmd]
    return _orig_run_command(cmd, *a, **kw)


_bass_utils.run_command = _run_command_ldwopt
from contextlib import ExitStack

F32 = mybir.dt.float32
BF16 = mybir.dt.bfloat16
I32 = mybir.dt.int32
ALU = mybir.AluOpType
AF = mybir.ActivationFunctionType
AX = mybir.AxisListType
BFnp = ml_dtypes.bfloat16

NCORES = 8
S, D = 16384, 1024
T = S // NCORES
NCH = T // 128
NHC = NCH + 1
NQB = T // 256
NH = 16
LH = 4
FFI = 2816
MAGIC = float(np.float32(3 * 2 ** 21))
C_EXP = 0x7F800000
C_RP = 0x7F000000


class Plan:
    ENGINES = ("sync", "scalar", "vector", "tensor", "gpsimd")

    def __init__(self):
        self.ops = []
        self.cnt = {}
        self.res = {}
        self.sems = set()

    def _sem(self, name):
        if name not in self.cnt:
            self.cnt[name] = 0
            self.sems.add(name)
        return name

    def op(self, eng, emit, reads=(), writes=(), sem=None, inc=1,
           serial=False):
        sem = self._sem(sem or ("s_" + eng))
        waits = {}

        def need(psem, v, peng):
            if v <= 0:
                return
            if peng == "tensor" and eng == "tensor":
                return
            waits[psem] = max(waits.get(psem, 0), v)

        for r in reads:
            rec = self.res.get(r)
            if rec and rec["w"]:
                need(*rec["w"])
        for w in writes:
            rec = self.res.get(w)
            if rec:
                if rec["w"]:
                    need(*rec["w"])
                for dep in rec["r"]:
                    need(*dep)
        self.cnt[sem] += inc
        me = (sem, self.cnt[sem], eng)
        for r in reads:
            self.res.setdefault(r, {"w": None, "r": []})["r"].append(me)
        for w in writes:
            self.res[w] = {"w": me, "r": []}
        post_self = (sem, self.cnt[sem]) if serial else None
        self.ops.append((eng, emit, waits, sem, inc, post_self))
        return me

    def rebind(self, resources, token):
        w = self.res[token]["w"]
        for r in resources:
            self.res[r] = {"w": w, "r": []}

    def fence(self, eng, sem, name):
        sem = self._sem(sem)
        fs = self._sem("f_" + name)
        cur = self.cnt[sem]
        self.cnt[fs] += 1
        fv = self.cnt[fs]

        def emit(e, sems):
            e.wait_ge(sems[sem], cur)
            e.sem_inc(sems[fs], 1)

        tok = "tok_" + name + str(fv)
        self.res[tok] = {"w": (fs, fv, eng), "r": []}
        self.ops.append((eng, ("RAW", emit), {}, None, 0, None))
        return tok

    def barrier(self):
        snap = dict(self.cnt)
        for eng in self.ENGINES:
            def emit(e, sems, snap=snap):
                for s_, v in sorted(snap.items()):
                    if v > 0:
                        e.wait_ge(sems[s_], v)
            self.ops.append((eng, ("RAW", emit), {}, None, 0, None))

    def emit_all(self, block, sems):
        per = {e: [] for e in self.ENGINES}
        for rec in self.ops:
            per[rec[0]].append(rec)

        def run_engine(e_obj, eng):
            hi = {}
            for (_, emit, waits, sem, inc, post_self) in per[eng]:
                if isinstance(emit, tuple) and emit[0] == "RAW":
                    emit[1](e_obj, sems)
                    continue
                for wsem, wval in sorted(waits.items()):
                    if hi.get(wsem, 0) < wval:
                        e_obj.wait_ge(sems[wsem], wval)
                        hi[wsem] = wval
                ins = emit(e_obj)
                if sem is not None:
                    ins.then_inc(sems[sem], inc)
                if post_self is not None:
                    e_obj.wait_ge(sems[post_self[0]], post_self[1])
                    hi[post_self[0]] = max(hi.get(post_self[0], 0),
                                           post_self[1])

        block.sync(lambda e: run_engine(e, "sync"))
        block.scalar(lambda e: run_engine(e, "scalar"))
        block.vector(lambda e: run_engine(e, "vector"))
        block.tensor(lambda e: run_engine(e, "tensor"))
        block.gpsimd(lambda e: run_engine(e, "gpsimd"))


def make_local_mask(first_block_core0):
    kk = np.arange(384)[:, None]
    qi = np.arange(256)[None, :]
    m = (kk >= qi) & (kk <= qi + 127) & (kk >= 1)
    if first_block_core0:
        m &= kk >= 128
    out = np.zeros((128, 768), dtype=BFnp)
    for c3 in range(3):
        out[:, c3 * 256:(c3 + 1) * 256] = m[c3 * 128:(c3 + 1) * 128].astype(BFnp)
    return out


def make_lat_mask(core):
    tok = np.arange(T) + core * T
    cid = tok // 256
    return (np.arange(64)[:, None] < cid[None, :]).astype(BFnp)


def build_program():
    nc = bass.Bass("TRN2", target_bir_lowering=False)
    P = Plan()

    def din(name, shape, dt):
        return nc.dram_tensor(name, list(shape), dt, kind="ExternalInput").ap()

    x_d = din("x", [NHC * 128, D], F32)
    qkvw_d = din("qkv_w", [D, 3 * D], F32)
    rqw_d = din("rq_w", [D, 256], F32)
    rkvw_d = din("rkv_w", [D, 512], F32)
    routw_d = din("rout_w", [256, D], F32)
    outw_d = din("out_w", [D, D], F32)
    glc_d = din("gl_col", [128, 1], F32)
    grc_d = din("gr2", [128, 2], F32)
    qb_d = din("qb_fm", [128, 8], F32)
    kb_d = din("kb_fm", [128, 8], F32)
    vb_d = din("vb_row", [1, D], F32)
    rqb_d = din("rqb_fm", [128, 2], F32)
    rkvb_d = din("rkvb_fm", [128, 4], F32)
    routb_d = din("routb_fm", [128, 8], F32)
    outb_d = din("outb_row", [1, D], F32)
    wg_sh_d = din("wg_sh", [352, D], F32)
    wu_sh_d = din("wu_sh", [352, D], F32)
    wd_sh_d = din("wd_sh", [128, FFI], F32)
    mask0_d = din("mask0", [128, 768], BF16)
    maskg_d = din("maskg", [128, 768], BF16)
    masklat_d = din("mask_lat", [64, T], BF16)
    ident_d = din("ident", [128, 128], BF16)

    y_d = nc.dram_tensor("y", [T, D], F32, kind="ExternalOutput").ap()
    xq_d = nc.dram_tensor("xq_scratch", [128, NCH * 1024], BF16,
                          kind="Internal").ap()
    wgu_c = nc.dram_tensor("wgu_c", [2048, 352], BF16, kind="Internal").ap()
    wgu_g = nc.dram_tensor("wgu_g", [2048 * NCORES, 352], BF16,
                           kind="Internal").ap()
    wd_c = nc.dram_tensor("wd_c", [FFI, 128], BF16, kind="Internal").ap()
    wd_g = nc.dram_tensor("wd_g", [FFI * NCORES, 128], BF16,
                          kind="Internal").ap()
    rkv_c = nc.dram_tensor("rkv_c", [8, 512], BF16, kind="Internal").ap()
    rkv_g = nc.dram_tensor("rkv_g", [64, 512], BF16, kind="Internal").ap()
    RG = [list(range(NCORES))]

    es = ExitStack()

    def sb(name, shape, dt, **kw):
        return es.enter_context(nc.sbuf_tensor(name, list(shape), dt, **kw))

    ident = sb("identsb", [128, 128], BF16)
    cvec = sb("cvec", [128, 3], F32)
    ones_r = sb("ones_r", [1, 128], BF16)
    pool = [sb(f"qp{i}", [128, 1024], F32) for i in range(3)]
    sc_a = sb("qsA", [128, 176], F32)
    sc_s = sb("qsS", [128, 176], F32)
    sc_r = sb("qsR", [128, 176], F32)
    gl_col = sb("glcolsb", [128, 1], BF16)
    gr2 = sb("gr2sb", [128, 2], BF16)
    qb_fm = sb("qbsb", [128, 8], F32)
    kb_fm = sb("kbsb", [128, 8], F32)
    vb_row = sb("vbsb", [1, D], BF16)
    rqb_fm = sb("rqbsb", [128, 2], F32)
    rkvb_fm = sb("rkvbsb", [128, 4], F32)
    routb_fm = sb("routbsb", [128, 8], F32)
    outb_row = sb("outbsb", [1, D], BF16)

    psE = es.enter_context(nc.psum_tensor("psE", [128, 1024], F32))
    psO = es.enter_context(nc.psum_tensor("psO", [128, 1024], F32))
    p4 = es.enter_context(nc.psum_tensor("p4", [128, 512], F32))
    p5 = es.enter_context(nc.psum_tensor("p5", [128, 512], F32))
    ptr = es.enter_context(nc.psum_tensor("ptr", [128, 2048], BF16))

    def dma(q, out, in_, reads, writes, serial=True):
        P.op(q, lambda e, o=out, i=in_: e.dma_start(out=o, in_=i),
             reads=reads, writes=writes, sem="d_" + q, inc=16, serial=serial)

    def cp_s(out, in_, reads, writes):
        P.op("scalar", lambda e, o=out, i=in_: e.copy(o, i),
             reads=reads, writes=writes)

    def cp_v(out, in_, reads, writes):
        P.op("vector", lambda e, o=out, i=in_: e.tensor_copy(o, i),
             reads=reads, writes=writes)

    def act(out, in_, func, reads, writes, bias=0.0, scale=1.0):
        P.op("scalar",
             lambda e, o=out, i=in_, f=func, b=bias, s=scale:
             e.activation(o, i, f, bias=b, scale=s),
             reads=reads, writes=writes)

    def tt(out, a, b, op, reads, writes):
        P.op("vector",
             lambda e, o=out, x=a, y=b, p=op: e.tensor_tensor(o, x, y, p),
             reads=reads, writes=writes)

    def mm(out, lhsT, rhs, start, stop, reads, writes, tp=None):
        P.op("tensor",
             lambda e, o=out, l=lhsT, r=rhs, st=start, sp=stop, t=tp:
             e.matmul(o, l, r, start=st, stop=sp, tile_position=t),
             reads=reads, writes=writes)

    def trans(out, in_, idn, reads, writes):
        P.op("tensor",
             lambda e, o=out, i=in_, d=idn:
             e.matmul(o, i, d, start=True, stop=True),
             reads=reads, writes=writes)

    def memset(ap_, val, writes, reads=()):
        P.op("vector", lambda e, a=ap_, v=val: e.memset(a, v), reads=reads,
             writes=writes)

    def quant(src, dst, p, w, reads, writes):
        nb = w // 16
        b0, b1, b2 = (x[0:p, 0:w] for x in pool)
        A, Ssc, R = sc_a[0:p, 0:nb], sc_s[0:p, 0:nb], sc_r[0:p, 0:nb]
        r_src = tuple(reads)
        P.op("vector", lambda e, o=A, i=src: e.tensor_reduce(
            o, i.rearrange("p (b w) -> p b w", w=16), AX.X, ALU.max,
            apply_absolute_value=True), reads=r_src, writes=("qsA",))
        P.op("vector", lambda e, o=Ssc, i=A: e.tensor_scalar(
            o, i, float(np.float32(1.0) / np.float32(6.0)), 1e-12,
            ALU.mult, ALU.max),
            reads=("qsA",), writes=("qsS",))
        P.op("vector", lambda e, o=R, i=Ssc: e.reciprocal(o, i),
             reads=("qsS",), writes=("qsR",))
        rbc = R.unsqueeze(2).broadcast_to([p, nb, 16])
        sbc = Ssc.unsqueeze(2).broadcast_to([p, nb, 16])
        v3 = src.rearrange("p (b w) -> p b w", w=16)
        P.op("vector", lambda e, o=b0.rearrange("p (b w) -> p b w", w=16),
             i=v3, r=rbc: e.tensor_tensor(o, i, r, ALU.mult),
             reads=r_src + ("qsR",), writes=("qp0",))
        P.op("vector", lambda e, o=b1.bitcast(I32), i=b0.bitcast(I32):
             e.tensor_scalar(o, i, C_EXP, None, ALU.bitwise_and),
             reads=("qp0",), writes=("qp1",))
        P.op("vector", lambda e, o=b1, i=b1: e.tensor_scalar(
            o, i, 1.0, 4.0, ALU.max, ALU.min),
            reads=("qp1",), writes=("qp1",))
        P.op("vector", lambda e, o=b2.bitcast(I32), i=b1.bitcast(I32):
             e.tensor_scalar(o, i, C_RP, -1, ALU.subtract, ALU.mult),
             reads=("qp1",), writes=("qp2",))
        P.op("vector", lambda e, o=b2, i=b0, r=b2: e.tensor_tensor(
            o, i, r, ALU.mult), reads=("qp0", "qp2"), writes=("qp2",))
        P.op("vector", lambda e, o=b0, i=b2: e.tensor_scalar(
            o, i, MAGIC, MAGIC, ALU.add, ALU.subtract),
            reads=("qp2",), writes=("qp0",))
        P.op("vector", lambda e, o=b2, i=b0, r=b1: e.tensor_tensor(
            o, i, r, ALU.mult), reads=("qp0", "qp1"), writes=("qp2",))
        P.op("vector", lambda e, o=dst.rearrange("p (b w) -> p b w", w=16),
             i=b2.rearrange("p (b w) -> p b w", w=16), r=sbc:
             e.tensor_tensor(o, i, r, ALU.mult),
             reads=("qp2", "qsS"), writes=tuple(writes))

    dma("sync", ident[:], ident_d, (), ("c.ident",))
    memset(ones_r[:], 1.0, ("c.ones",))
    memset(cvec[:, 0:1], 1e-5, ("c.cv0",))
    memset(cvec[:, 1:2], 1e-30, ("c.cv1",))
    memset(cvec[:, 2:3], 1.0, ("c.cv2",))
    dma("gpsimd", gl_col[:], glc_d, (), ("c.gl",))
    dma("gpsimd", gr2[:], grc_d, (), ("c.gr",))
    dma("sync", qb_fm[:], qb_d, (), ("c.qb",))
    dma("sync", kb_fm[:], kb_d, (), ("c.kb",))
    dma("gpsimd", vb_row[:], vb_d, (), ("c.vb",))
    dma("sync", rqb_fm[:], rqb_d, (), ("c.rqb",))
    dma("sync", rkvb_fm[:], rkvb_d, (), ("c.rkvb",))
    dma("sync", routb_fm[:], routb_d, (), ("c.routb",))
    dma("gpsimd", outb_row[:], outb_d, (), ("c.outb",))

    esB = ExitStack()

    def sbB(name, shape, dt, **kw):
        return esB.enter_context(nc.sbuf_tensor(name, list(shape), dt, **kw))

    x_roll = sbB("x_roll", [128, 2 * 1024], F32)
    x_res = sbB("x_res", [128, 1024], F32)
    xn_fm = sbB("xn_fm", [128, 8 * NHC * 128], BF16)
    xn_tm = [x_roll[:, 0:1024], x_roll[:, 1024:2048]]
    xn_bf = [sbB(f"xn_bf{i}", [128, 1024], BF16) for i in range(2)]
    xq_tm = xn_bf
    mv = [sbB(f"mv{i}", [128, 2], F32) for i in range(2)]
    bnst = [sbB(f"bnst{i}", [128, 12], F32) for i in range(2)]
    rstd = [sbB(f"rstd{i}", [128, 1], F32) for i in range(2)]
    masks = sbB("masks", [128, 2 * 768], BF16)
    masklat = sbB("masklat", [64, T], BF16)
    pooled_bf = sbB("pooled_bf", [128, 64], BF16)
    rk_fm = sbB("rk_fm", [128, 128], BF16)
    rv_l = sbB("rv_l", [64, 4 * 65], BF16)
    rkv_bf = sbB("rkv_bf", [128, 32], BF16)
    xqT_st = sbB("xqT_st", [128, 1024], BF16)

    dma("sync", masks[:, 0:768], mask0_d, (), ("c.mask0",))
    dma("sync", masks[:, 768:1536], maskg_d, (), ("c.maskg",))
    dma("sync", masklat[:], masklat_d, (), ("c.masklat",))

    esSh = ExitStack()
    w_st = esSh.enter_context(nc.sbuf_tensor("w_st", [128, 1024], F32))
    shq = esSh.enter_context(nc.sbuf_tensor("shq", [128, 1024], BF16))
    trst = esSh.enter_context(nc.sbuf_tensor("trst", [128, 1024], BF16))

    ptrF32 = ptr[:].bitcast(F32)

    def PTR(sl, w=128, p=128):
        s = (sl % 2) * 512
        return ptrF32[0:p, s:s + w]

    for wsrc, drow0 in ((wg_sh_d, 0), (wu_sh_d, 1024)):
        for rt, rows_n in enumerate((128, 128, 96)):
            dma("sync", w_st[0:rows_n, :], wsrc[rt * 128:rt * 128 + rows_n, :],
                (), ("w_st",))
            quant(w_st[0:rows_n, :], shq[0:rows_n, :], rows_n, 1024,
                  ("w_st",), ("shq",))
            for grp in range(2):
                j0, sl = grp * 4, grp
                for j in range(4):
                    kc = j0 + j
                    trans(ptrF32[0:128, sl * 512 + j * rows_n:
                                 sl * 512 + (j + 1) * rows_n],
                          shq[0:rows_n, kc * 128:(kc + 1) * 128],
                          ident[0:rows_n, 0:rows_n],
                          reads=("shq", "c.ident"), writes=(f"ptr.s{sl}",))
                cp_s(trst[:, sl * 512:sl * 512 + 4 * rows_n],
                     ptrF32[0:128, sl * 512:sl * 512 + 4 * rows_n],
                     (f"ptr.s{sl}",), (f"trst.s{sl}",))
                dma("scalar",
                    wgu_c[drow0 + j0 * 128:drow0 + (j0 + 4) * 128,
                          rt * 128:rt * 128 + rows_n]
                    .rearrange("(kc p) n -> p kc n", p=128),
                    trst[:, sl * 512:sl * 512 + 4 * rows_n]
                    .rearrange("p (kc n) -> p kc n", n=rows_n),
                    (f"trst.s{sl}",),
                    (f"wguc.{drow0}.{rt}.{j0}",))
    for half, (hc0, w) in enumerate(((0, 1024), (1024, 1024), (2048, 768))):
        dma("sync", w_st[:, 0:w], wd_sh_d[:, hc0:hc0 + w], (), ("w_st",))
        quant(w_st[:, 0:w], shq[:, 0:w], 128, w, ("w_st",), ("shq",))
        nj = w // 128
        for grp in range((nj + 3) // 4):
            j0 = grp * 4
            j1 = min(j0 + 4, nj)
            nb = j1 - j0
            sl = grp % 2
            for j in range(j0, j1):
                trans(ptrF32[0:128, sl * 512 + (j - j0) * 128:
                             sl * 512 + (j - j0 + 1) * 128],
                      shq[:, j * 128:(j + 1) * 128], ident[:],
                      reads=("shq", "c.ident"), writes=(f"ptr.s{sl}",))
            cp_s(trst[:, sl * 512:sl * 512 + nb * 128],
                 ptrF32[0:128, sl * 512:sl * 512 + nb * 128],
                 (f"ptr.s{sl}",), (f"trst.s{sl}",))
            kc0 = hc0 // 128 + j0
            dma("scalar",
                wd_c[kc0 * 128:(kc0 + nb) * 128, :]
                .rearrange("(kc p) n -> p kc n", p=128),
                trst[:, sl * 512:sl * 512 + nb * 128]
                .rearrange("p (kc n) -> p kc n", n=128),
                (f"trst.s{sl}",), (f"wdc.{kc0}",))
    shard_tok = P.fence("scalar", "d_scalar", "shardw")
    P.op("gpsimd", lambda e: e.collective_compute(
        "AllGather", ALU.bypass, replica_groups=RG,
        ins=[wgu_c.opt()], outs=[wgu_g.opt()]),
        reads=(shard_tok,), writes=("wgu_g",), sem="col_wgu", inc=1)
    P.op("gpsimd", lambda e: e.collective_compute(
        "AllGather", ALU.bypass, replica_groups=RG,
        ins=[wd_c.opt()], outs=[wd_g.opt()]),
        reads=(shard_tok,), writes=("wd_g",), sem="col_wd", inc=1)
    esSh.close()

    qkv_fm = sbB("qkv_fm", [128, 8 * 3072], BF16)
    rqw_fm = sbB("rqw_fm", [128, 8 * 256], BF16)
    routw_fm = sbB("routw_fm", [128, 2 * 1024], BF16)
    outw_fm = sbB("outw_fm", [128, 8 * 1024], BF16)
    esW = ExitStack()
    rkvw_fm = esW.enter_context(
        nc.sbuf_tensor("rkvw_fm", [128, 8 * 512], BF16))
    contrib = esW.enter_context(nc.sbuf_tensor("contrib", [8, 512], BF16))
    gath = esW.enter_context(nc.sbuf_tensor("gath", [64, 512], BF16))

    for dst, srcw, kcn, rname in (
            (qkv_fm, qkvw_d, 8, "w.qkv"), (rqw_fm, rqw_d, 8, "w.rq"),
            (rkvw_fm, rkvw_d, 8, "w.rkv"), (routw_fm, routw_d, 2, "w.rout"),
            (outw_fm, outw_d, 8, "w.out")):
        dma("gpsimd", dst[:].rearrange("p (kc n) -> p kc n", kc=kcn),
            srcw.rearrange("(kc p) n -> p kc n", p=128), (), (rname,),
            serial=False)
    wtok = P.fence("gpsimd", "d_gpsimd", "wts")
    P.rebind(("w.qkv", "w.rq", "w.rkv", "w.rout", "w.out"), wtok)

    def xcol(kc, hc):
        return kc * (NHC * 128) + hc * 128

    def c_stage1(hc):
        par = hc % 2
        if hc == 0:
            P.op("sync", lambda e, o=x_roll[:, 0:1024], i=x_d[0:128, :]:
                 e.dma_start(out=o, in_=i), reads=(), writes=("xroll.0",),
                 sem="d_x0", inc=16)
        if hc + 1 < NHC:
            npar = (hc + 1) % 2
            P.op("sync", lambda e,
                 o=x_roll[:, npar * 1024:(npar + 1) * 1024],
                 i=x_d[(hc + 1) * 128:(hc + 2) * 128, :]:
                 e.dma_start(out=o, in_=i), reads=(),
                 writes=(f"xroll.{npar}",), sem=f"d_x{npar}", inc=16)
        xr = x_roll[:, par * 1024:(par + 1) * 1024]
        rx = f"xroll.{par}"
        P.op("vector", lambda e, o=bnst[par][:, 0:6], i=xr[:, 0:512]:
             e.bn_stats(o, i), reads=(rx,), writes=(f"bn0.{par}",))
        P.op("vector", lambda e, o=bnst[par][:, 6:12], i=xr[:, 512:1024]:
             e.bn_stats(o, i), reads=(rx,), writes=(f"bn1.{par}",))
        P.op("vector", lambda e, o=mv[par][:], i=bnst[par][:]:
             e.bn_aggr(o, i),
             reads=(f"bn0.{par}", f"bn1.{par}"), writes=(f"mv.{par}",))
        act(rstd[par][:], mv[par][:, 1:2], AF.Ln,
            (f"mv.{par}", "c.cv0"), (f"rstd.{par}",), bias=cvec[:, 0:1])
        act(rstd[par][:], rstd[par][:], AF.Exp, (f"rstd.{par}",),
            (f"rstd.{par}",), scale=-0.5)
        P.op("vector", lambda e, o=xn_tm[par], i=xr, m=mv[par][:, 0:1],
             r=rstd[par][:]:
             e.tensor_scalar(o, i, m, r, ALU.subtract, ALU.mult),
             reads=(rx, f"mv.{par}", f"rstd.{par}"), writes=(rx,))
        cp_s(xn_bf[par][:], xn_tm[par], (rx,),
             (f"xn_bf{par}",))
        for grp in range(2):
            j0, sl = grp * 4, grp
            for j in range(4):
                kc = j0 + j
                trans(ptrF32[0:128, sl * 512 + j * 128:
                             sl * 512 + (j + 1) * 128],
                      xn_bf[par][:, kc * 128:(kc + 1) * 128], ident[:],
                      reads=(f"xn_bf{par}", "c.ident"),
                      writes=(f"ptr.s{sl}",))
            cp_s(xn_fm[:].rearrange("p (kc t) -> p kc t", kc=8)
                 [:, j0:j0 + 4, hc * 128:(hc + 1) * 128],
                 ptrF32[0:128, sl * 512:sl * 512 + 512]
                 .rearrange("p (kc t) -> p kc t", kc=4),
                 (f"ptr.s{sl}",),
                 tuple(f"xnfm.{kc}.{hc}" for kc in range(j0, j0 + 4)))

    def c_quant(hc):
        par = hc % 2
        t = hc - 1
        quant(xn_tm[par], xq_tm[par][:], 128, 1024,
              (f"xroll.{par}",), (f"xn_bf{par}",))
        if t % 2 == 1:
            pi = t // 2
            P.op("vector", lambda e,
                 o=p4[0:128, pi * 8:(pi + 1) * 8],
                 i=xn_fm[:].rearrange("p (kc c) -> p kc c", kc=8)
                 [:, :, t * 128:(t + 2) * 128]:
                 e.tensor_reduce(o, i, AX.X, ALU.add),
                 reads=tuple(f"xnfm.{kc}.{hc}" for kc in range(8)
                             for hc in (t, t + 1)),
                 writes=("p4",))

    def c_stage2(hc):
        par = hc % 2
        t = hc - 1
        for grp in range(2):
            j0, sl = grp * 4, grp
            for j in range(4):
                kc = j0 + j
                trans(ptrF32[0:128, sl * 512 + j * 128:
                             sl * 512 + (j + 1) * 128],
                      xq_tm[par][:, kc * 128:(kc + 1) * 128], ident[:],
                      reads=(f"xn_bf{par}", "c.ident"),
                      writes=(f"ptr.s{sl}",))
            cp_s(xqT_st[:, j0 * 128:(j0 + 4) * 128],
                 ptrF32[0:128, sl * 512:sl * 512 + 512],
                 (f"ptr.s{sl}",), ("xqT_st",))
        P.op("scalar", lambda e, o=xq_d[:, t * 1024:(t + 1) * 1024],
             i=xqT_st[:]: e.dma_start(out=o, in_=i),
             reads=("xqT_st",), writes=(f"xqd.{t}",), sem="d_xqst", inc=16)

    for i in range(NHC + 2):
        if 2 <= i <= NHC:
            c_quant(i - 1)
        if 3 <= i <= NHC + 1:
            c_stage2(i - 2)
        if i < NHC:
            c_stage1(i)

    act(pooled_bf[:].rearrange("p (kc pi) -> p kc pi", pi=8),
        p4[0:128, 0:64].rearrange("p (pi kc) -> p kc pi", kc=8),
        AF.Identity, ("p4",), ("pooled_bf",), scale=1.0 / 256.0)
    for mc in range(4):
        bank, bres = (psE, "E0") if mc % 2 == 0 else (psO, "O0")
        for kc in range(8):
            mm(bank[0:128, mc * 8:(mc + 1) * 8],
               rkvw_fm[:, kc * 512 + mc * 128:kc * 512 + (mc + 1) * 128],
               pooled_bf[:, kc * 8:(kc + 1) * 8],
               start=(kc == 0), stop=(kc == 7),
               reads=("pooled_bf", "w.rkv", wtok), writes=(bres,))
        P.op("scalar", lambda e, o=rkv_bf[:, mc * 8:(mc + 1) * 8],
             i=bank[0:128, mc * 8:(mc + 1) * 8], b=rkvb_fm[:, mc:mc + 1]:
             e.activation(o, i, AF.Identity, bias=b),
             reads=(bres, "c.rkvb"), writes=(f"rkvbf.{mc}",))
    for mc in range(4):
        trans(ptrF32[0:8, mc * 128:(mc + 1) * 128],
              rkv_bf[:, mc * 8:(mc + 1) * 8], ident[:],
              reads=(f"rkvbf.{mc}", "c.ident"), writes=("ptr.s0",))
    cp_s(contrib[:], ptrF32[0:8, 0:512], ("ptr.s0",),
         tuple(f"contrib.{mc}" for mc in range(4)))
    dma("gpsimd", rkv_c, contrib[:],
        tuple(f"contrib.{mc}" for mc in range(4)), ("rkv_c",))
    P.op("gpsimd", lambda e: e.collective_compute(
        "AllGather", ALU.bypass, replica_groups=RG,
        ins=[rkv_c.opt()], outs=[rkv_g.opt()]),
        reads=("rkv_c",), writes=("rkv_g",), sem="col_rkv", inc=1)

    esW.close()

    k_roll = sbB("k_roll", [128, 8 * 3 * 128], BF16)
    v_roll = sbB("v_roll", [128, 3 * 16 * 65], BF16)
    q_fm = sbB("q_fm", [128, 8 * 256], BF16)
    rq_fm = sbB("rq_fm", [128, 4 * 512], BF16)
    e_raw = [sbB(f"e_raw{i}", [128, 768], BF16) for i in range(4)]
    u_bf = sbB("u_bf", [128, 4 * 256], BF16)
    Dt = [sbB(f"Dt{i}", [128, 256], F32) for i in range(2)]
    rdt0 = sbB("rdt0", [128, 256], F32)
    z10 = sbB("z10", [128, 256], F32)
    w10 = sbB("w10", [128, 256], F32)
    rdt = [rdt0, rdt0]
    z1 = [z10, z10]
    w1 = [w10, w10]
    a4 = [sbB(f"a4{i}", [128, 256], BF16) for i in range(2)]
    b4 = [sbB(f"b4{i}", [128, 256], BF16) for i in range(2)]
    ones2 = sbB("ones2", [128, 64], BF16)
    lat_row = sbB("lat_row", [1, 512], F32)
    lat_rbf = sbB("lat_rbf", [1, 512], BF16)
    a_bf = sbB("a_bf", [128, 256], BF16)
    b_bf = sbB("b_bf", [128, 256], BF16)
    latout_fm = sbB("latout_fm", [128, 2 * 2048], BF16)
    remote_fm = sbB("remote_fm", [128, 8 * 256], BF16)
    mixed_fm = sbB("mixed_fm", [128, 8 * 256], BF16)

    memset(ones2[:], 1.0, ("c.ones2",))
    for i in range(2):
        memset(Dt[i][:], 0.0, (f"Dt{i}",))
        P.op("vector", lambda e, o=ptrF32[:, i * 512:(i + 1) * 512]:
             e.memset(o, 0.0), reads=(), writes=(f"ptr.s{i}",))

    def kv_bank(i):
        return (psE, "E0") if i % 2 == 0 else (psO, "O0")

    def produce_k_one(hc):
        slot = hc % 3
        xn_r = tuple(f"xnfm.{kc}.{hc}" for kc in range(8))
        for p in range(8):
            bank, bres = kv_bank(p)
            for kc in range(8):
                mm(bank[0:128, 0:128],
                   qkv_fm[:, kc * 3072 + 1024 + p * 128:
                          kc * 3072 + 1024 + (p + 1) * 128],
                   xn_fm[:, xcol(kc, hc):xcol(kc, hc) + 128],
                   start=(kc == 0), stop=(kc == 7),
                   reads=xn_r + ("w.qkv",), writes=(bres,))
            P.op("vector", lambda e,
                 o=k_roll[:, (p * 3 + slot) * 128:(p * 3 + slot) * 128 + 128],
                 i=bank[0:128, 0:128], b=kb_fm[:, p:p + 1]:
                 e.tensor_scalar(o, i, b, None, ALU.add),
                 reads=(bres, "c.kb"), writes=(f"kroll.{p}.{slot}",))

    def produce_k_pair(hc1):
        s1, s2 = hc1 % 3, (hc1 + 1) % 3
        xn_r = tuple(f"xnfm.{kc}.{hc}" for kc in range(8)
                     for hc in (hc1, hc1 + 1))
        for p in range(8):
            bank, bres = kv_bank(p)
            for kc in range(8):
                mm(bank[0:128, 0:256],
                   qkv_fm[:, kc * 3072 + 1024 + p * 128:
                          kc * 3072 + 1024 + (p + 1) * 128],
                   xn_fm[:, xcol(kc, hc1):xcol(kc, hc1) + 256],
                   start=(kc == 0), stop=(kc == 7),
                   reads=xn_r + ("w.qkv",), writes=(bres,))
            for ci, slot in ((0, s1), (1, s2)):
                P.op("vector", lambda e,
                     o=k_roll[:, (p * 3 + slot) * 128:
                              (p * 3 + slot) * 128 + 128],
                     i=bank[0:128, ci * 128:(ci + 1) * 128],
                     b=kb_fm[:, p:p + 1]:
                     e.tensor_scalar(o, i, b, None, ALU.add),
                     reads=(bres, "c.kb"), writes=(f"kroll.{p}.{slot}",))

    def produce_v(hc):
        slot = hc % 3
        xn_r = tuple(f"xnfm.{kc}.{hc}" for kc in range(8))
        for vt in range(2):
            bank, bres = kv_bank(vt)
            for kc in range(8):
                mm(bank[:, 0:512],
                   xn_fm[:, xcol(kc, hc):xcol(kc, hc) + 128],
                   qkv_fm[:, kc * 3072 + 2048 + vt * 512:
                          kc * 3072 + 2048 + (vt + 1) * 512],
                   start=(kc == 0), stop=False,
                   reads=xn_r + ("w.qkv",), writes=(bres,))
            mm(bank[:, 0:512], ones_r[0:1, 0:128],
               vb_row[0:1, vt * 512:(vt + 1) * 512],
               start=False, stop=True, reads=("c.ones", "c.vb"),
               writes=(bres,))
            P.op("vector", lambda e,
                 o=v_roll[:, (slot * 16 + vt * 8) * 65:
                          (slot * 16 + vt * 8 + 8) * 65].rearrange(
                     "p (h c) -> p h c", c=65)[:, :, 0:64],
                 i=bank[:, 0:512].rearrange("p (h c) -> p h c", c=64):
                 e.tensor_copy(o, i),
                 reads=(bres,), writes=(f"vroll.{vt}.{slot}",))

    def qblock(b):
        hcs = (2 * b, 2 * b + 1, 2 * b + 2)
        if b == 0:
            produce_k_one(0)
            produce_v(0)
        produce_k_pair(2 * b + 1)
        produce_v(2 * b + 1)
        produce_v(2 * b + 2)
        kv_r = tuple(f"kroll.{p}.{hc % 3}" for p in range(8) for hc in hcs) + \
            tuple(f"vroll.{vt}.{hc % 3}" for vt in range(2) for hc in hcs)
        for p in range(8):
            bank, bres = kv_bank(p)
            for kc in range(8):
                mm(bank[0:128, 0:256],
                   qkv_fm[:, kc * 3072 + p * 128:kc * 3072 + (p + 1) * 128],
                   xn_fm[:, xcol(kc, 2 * b + 1):xcol(kc, 2 * b + 1) + 256],
                   start=(kc == 0), stop=(kc == 7),
                   reads=(f"xnfm.{kc}.{2 * b + 1}", f"xnfm.{kc}.{2 * b + 2}",
                          "w.qkv"), writes=(bres,))
            P.op("vector", lambda e, o=q_fm[:, p * 256:(p + 1) * 256],
                 i=bank[0:128, 0:256], b=qb_fm[:, p:p + 1]:
                 e.tensor_scalar(o, i, b, None, ALU.add),
                 reads=(bres, "c.qb"), writes=(f"qfm.{p}",))
        lat_r = tuple(f"latout.{lh}.{b // 2}" for lh in range(LH))
        for mc in range(8):
            bank, bres = kv_bank(mc)
            for kc in range(2):
                mm(bank[0:128, 0:256],
                   routw_fm[:, kc * 1024 + mc * 128:kc * 1024 + (mc + 1) * 128],
                   latout_fm[:, kc * 2048 + b * 256:
                             kc * 2048 + (b + 1) * 256],
                   start=(kc == 0), stop=(kc == 1),
                   reads=lat_r + ("w.rout",), writes=(bres,))
            P.op("vector", lambda e, o=remote_fm[:, mc * 256:(mc + 1) * 256],
                 i=bank[0:128, 0:256], b=routb_fm[:, mc:mc + 1]:
                 e.tensor_scalar(o, i, b, None, ALU.add),
                 reads=(bres, "c.routb"), writes=(f"remote.{mc}",))
        mres = "c.mask0" if b == 0 else "c.maskg"
        moff = 0 if b == 0 else 768

        def head_scores(h):
            par = h % 2
            p = h // 2
            ph = par * 64
            psH = psE if par == 0 else psO
            bn0 = "E0" if par == 0 else "O0"
            bn1 = "E1" if par == 0 else "O1"
            for c3 in range(3):
                slot = hcs[c3] % 3
                dst = (psH[0:128, c3 * 256:(c3 + 1) * 256] if c3 < 2
                       else psH[0:128, 512:768])
                mm(dst,
                   k_roll[ph:ph + 64,
                          (p * 3 + slot) * 128:(p * 3 + slot) * 128 + 128],
                   q_fm[ph:ph + 64, p * 256:(p + 1) * 256],
                   start=True, stop=True,
                   reads=kv_r + (f"qfm.{p}",),
                   writes=(bn0 if c3 < 2 else bn1,))

        def head_expmask(h):
            par = h % 2
            e4 = h % 4
            psH = psE if par == 0 else psO
            bn0 = "E0" if par == 0 else "O0"
            bn1 = "E1" if par == 0 else "O1"
            er = e_raw[e4]
            act(er[:], psH[0:128, 0:768], AF.Exp, (bn0, bn1),
                (f"eraw{e4}",), scale=0.125)
            tt(er[:], er[:], masks[:, moff:moff + 768], ALU.mult,
               (f"eraw{e4}", mres), (f"eraw{e4}",))

        def head_U(h):
            par = h % 2
            p = h // 2
            ph = par * 64
            g = h // 4
            j = h % 4
            gp = g % 2
            pr = (h % 4) // 2
            psH = psE if par == 0 else psO
            bn1 = "E1" if par == 0 else "O1"
            pU = p4 if par == 0 else p5
            bnU = "p4" if par == 0 else "p5"
            er = e_raw[h % 4]
            ptrF = ptrF32[:, gp * 512:(gp + 1) * 512]
            for c3 in range(3):
                slot = hcs[c3] % 3
                mm(pU[0:65, 0:256],
                   v_roll[:, (slot * 16 + h) * 65:(slot * 16 + h) * 65 + 65],
                   er[:, c3 * 256:(c3 + 1) * 256],
                   start=(c3 == 0), stop=(c3 == 2),
                   reads=(f"eraw{h % 4}",) + kv_r + ("vroll.ones",),
                   writes=(bnU,))
            ub0 = (gp * 2 + pr) * 256
            cp_s(u_bf[ph:ph + 64, ub0:ub0 + 256],
                 pU[0:64, 0:256], (bnU,), (f"ubf.{par}.{gp}.{pr}",))
            act(Dt[gp][32 * j:32 * j + 1, 0:256], pU[64:65, 0:256], AF.Ln,
                (bnU, "c.cv1"), (f"Dt{gp}",), bias=cvec[0:1, 1:2])
            mm(ptrF[32 * j:32 * j + 1, 0:256], gl_col[ph:ph + 64, 0:1],
               u_bf[ph:ph + 64, ub0:ub0 + 256],
               start=True, stop=True,
               reads=(f"ubf.{par}.{gp}.{pr}", "c.gl"),
               writes=(f"ptr.s{gp}",), tp=(ph, 32 * j))
            mm(ptrF[32 * j:32 * j + 1, 256:512], gr2[:, par:par + 1],
               remote_fm[:, p * 256:(p + 1) * 256],
               start=True, stop=True, reads=(f"remote.{p}", "c.gr"),
               writes=(f"ptr.s{gp}",), tp=(0, 32 * j))

        def group_chain(g):
            gp = g % 2
            ptrF = ptrF32[:, gp * 512:(gp + 1) * 512]
            pres = f"ptr.s{gp}"
            dres = f"Dt{gp}"
            act(rdt[gp][:], Dt[gp][:], AF.Exp, (dres,), ("rdt",),
                scale=-1.0)
            P.op("vector", lambda e, o=z1[gp][:], i=rdt[gp][:],
                 r=ptrF[:, 0:256]: e.tensor_tensor(o, i, r, ALU.mult),
                 reads=("rdt", pres), writes=("z1",))
            P.op("vector", lambda e, o=z1[gp][:], i=z1[gp][:],
                 r=ptrF[:, 256:512]: e.tensor_tensor(o, i, r, ALU.add),
                 reads=("z1", pres), writes=("z1",))
            act(w1[gp][:], z1[gp][:], AF.Exp, ("z1",), ("w1",),
                scale=-1.0)
            act(rdt[gp][:], w1[gp][:], AF.Ln, ("w1", "c.cv2"),
                ("rdt",), bias=cvec[:, 2:3])
            tt(w1[gp][:], Dt[gp][:], rdt[gp][:], ALU.add,
               (dres, "rdt"), ("w1",))
            act(a4[gp][:], w1[gp][:], AF.Exp, ("w1",), (f"a4{gp}",),
                scale=-1.0)
            tt(w1[gp][:], z1[gp][:], rdt[gp][:], ALU.add,
               ("z1", "rdt"), ("w1",))
            act(b4[gp][:], w1[gp][:], AF.Exp, ("w1",), (f"b4{gp}",),
                scale=-1.0)

        def pair_mix(g, pr):
            gp = g % 2
            p = 2 * g + pr
            jA = 2 * pr
            jB = 2 * pr + 1
            psH = psE if pr == 0 else psO
            bn0 = "E0" if pr == 0 else "O0"
            bn1 = "E1" if pr == 0 else "O1"
            mm(psH[0:64, 768:1024], ones2[32 * jA:32 * jA + 1, 0:64],
               a4[gp][32 * jA:32 * jA + 1, 0:256],
               start=True, stop=True, reads=(f"a4{gp}", "c.ones2"),
               writes=(bn1,), tp=(32 * jA, 0))
            mm(psH[64:128, 768:1024], ones2[32 * jB:32 * jB + 1, 0:64],
               a4[gp][32 * jB:32 * jB + 1, 0:256],
               start=True, stop=True, reads=(f"a4{gp}", "c.ones2"),
               writes=(bn1,), tp=(32 * jB, 64))
            pz, bz = (p4, "p4") if pr == 0 else (p5, "p5")
            mm(pz[0:64, 256:512], ones2[32 * jA:32 * jA + 1, 0:64],
               b4[gp][32 * jA:32 * jA + 1, 0:256],
               start=True, stop=True, reads=(f"b4{gp}", "c.ones2"),
               writes=(bz,), tp=(32 * jA, 0))
            mm(pz[64:128, 256:512], ones2[32 * jB:32 * jB + 1, 0:64],
               b4[gp][32 * jB:32 * jB + 1, 0:256],
               start=True, stop=True, reads=(f"b4{gp}", "c.ones2"),
               writes=(bz,), tp=(32 * jB, 64))
            cp_s(a_bf[:], psH[:, 768:1024], (bn1,), ("abf",))
            cp_s(b_bf[:], pz[:, 256:512], (bz,), ("bbf",))
            P.op("vector", lambda e,
                 o=mixed_fm[:, p * 256:(p + 1) * 256],
                 i=u_bf[:, (gp * 2 + pr) * 256:(gp * 2 + pr + 1) * 256],
                 r=a_bf[:]:
                 e.tensor_tensor(o, i, r, ALU.mult),
                 reads=(f"ubf.0.{gp}.{pr}", f"ubf.1.{gp}.{pr}", "abf"),
                 writes=(f"mixed.{p}",))
            P.op("vector", lambda e, o=b_bf[:],
                 i=remote_fm[:, p * 256:(p + 1) * 256], r=b_bf[:]:
                 e.tensor_tensor(o, i, r, ALU.mult),
                 reads=(f"remote.{p}", "bbf"), writes=("bbf",))
            P.op("vector", lambda e,
                 o=mixed_fm[:, p * 256:(p + 1) * 256],
                 i=mixed_fm[:, p * 256:(p + 1) * 256], r=b_bf[:]:
                 e.tensor_tensor(o, i, r, ALU.add),
                 reads=(f"mixed.{p}", "bbf"), writes=(f"mixed.{p}",))

        for pi_ in range(10):
            if pi_ < 8:
                head_scores(2 * pi_)
                head_scores(2 * pi_ + 1)
            if pi_ < 8:
                head_expmask(2 * pi_)
                head_expmask(2 * pi_ + 1)
            if 1 <= pi_ <= 8:
                for h in (2 * pi_ - 2, 2 * pi_ - 1):
                    head_U(h)
                    if h % 4 == 3:
                        group_chain(h // 4)
            if pi_ >= 3 and pi_ % 2 == 1:
                g = (pi_ - 3) // 2
                pair_mix(g, 0)
                pair_mix(g, 1)
        mix_r = tuple(f"mixed.{p}" for p in range(8))
        for half in range(2):
            t = 2 * b + half
            dma("sync", x_res[:], x_d[(t + 1) * 128:(t + 2) * 128, :], (),
                ("x_res",) + (("lat_u",) if b == 0 and half == 0 else ()))
            for nt in range(2):
                bank, bres = kv_bank(nt)
                for kc in range(8):
                    mm(bank[:, 0:512],
                       mixed_fm[:, kc * 256 + half * 128:
                                kc * 256 + half * 128 + 128],
                       outw_fm[:, kc * 1024 + nt * 512:
                               kc * 1024 + (nt + 1) * 512],
                       start=(kc == 0), stop=False,
                       reads=mix_r + ("w.out",), writes=(bres,))
                mm(bank[:, 0:512], ones_r[0:1, 0:128],
                   outb_row[0:1, nt * 512:(nt + 1) * 512],
                   start=False, stop=True, reads=("c.ones", "c.outb"),
                   writes=(bres,))
                P.op("vector", lambda e, o=x_res[:, nt * 512:(nt + 1) * 512],
                     i=bank[:, 0:512], r=x_res[:, nt * 512:(nt + 1) * 512]:
                     e.tensor_tensor(o, i, r, ALU.add),
                     reads=(bres, "x_res"), writes=("x_res",))
            P.op("scalar", lambda e, o=y_d[t * 128:(t + 1) * 128, :],
                 i=x_res[:]: e.dma_start(out=o, in_=i),
                 reads=("x_res",), writes=(f"y1d.{t}",), sem="d_y1st",
                 inc=16)

    lat_u = x_res[0:64, 512:1024]
    def rq_emit(nc4):
        sl4 = nc4 % 2
        for mc in range(2):
            bank, bres = kv_bank(mc)
            for kc in range(8):
                mm(bank[0:128, 0:512],
                   rqw_fm[:, kc * 256 + mc * 128:kc * 256 + (mc + 1) * 128],
                   xn_fm[:, xcol(kc, 4 * nc4 + 1):xcol(kc, 4 * nc4 + 1) + 512],
                   start=(kc == 0), stop=(kc == 7),
                   reads=tuple(f"xnfm.{kc}.{hc}"
                               for hc in range(4 * nc4 + 1, 4 * nc4 + 5))
                   + ("w.rq",), writes=(bres,))
            P.op("vector", lambda e,
                 o=rq_fm[:, (sl4 * 2 + mc) * 512:(sl4 * 2 + mc + 1) * 512],
                 i=bank[0:128, 0:512], b=rqb_fm[:, mc:mc + 1]:
                 e.tensor_scalar(o, i, b, None, ALU.add),
                 reads=(bres, "c.rqb"), writes=(f"rqfm.{sl4}.{mc}",))

    rq_emit(0)
    rq_emit(1)
    dma("gpsimd", gath[:], rkv_g, ("rkv_g",), ("gath",))
    for lh in range(LH):
        trans(ptrF32[0:64, lh * 64:(lh + 1) * 64],
              gath[:, lh * 64:lh * 64 + 64],
              ident[0:64, 0:64],
              reads=("gath", "c.ident"), writes=("ptr.s0",))
    for lh in range(LH):
        cp_s(rk_fm[(lh % 2) * 64:(lh % 2) * 64 + 64,
                   (lh // 2) * 64:(lh // 2) * 64 + 64],
             ptrF32[0:64, lh * 64:(lh + 1) * 64],
             ("ptr.s0",), (f"rkfm.{lh}",))
        cp_v(rv_l[:, lh * 65:lh * 65 + 64],
             gath[:, 256 + lh * 64:256 + lh * 64 + 64], ("gath",),
             (f"rvl.{lh}",))
    memset(rv_l[:, 64::65], 1.0, ("rvl.ones",))
    memset(v_roll[:, 64::65], 1.0,
           ("vroll.ones", "gath") + tuple(f"contrib.{mc}" for mc in range(4)))

    for nc4 in range(4):
        for lh in range(LH):
            it = nc4 * LH + lh
            pb = (lh % 2) * 64
            sbank, sres = (psE, "E0") if it % 2 == 0 else (psO, "O0")
            pU, bnU = (p4, "p4") if it % 2 == 0 else (p5, "p5")
            pB, bnB = (p5, "p5") if it % 2 == 0 else (p4, "p4")
            le = e_raw[it % 2][0:64, 0:512]
            ler = f"eraw{it % 2}"
            mm(sbank[0:64, 0:512],
               rk_fm[pb:pb + 64, (lh // 2) * 64:(lh // 2) * 64 + 64],
               rq_fm[pb:pb + 64,
                     ((nc4 % 2) * 2 + lh // 2) * 512:
                     ((nc4 % 2) * 2 + lh // 2 + 1) * 512],
               start=True, stop=True,
               reads=(f"rkfm.{lh}", f"rqfm.{nc4 % 2}.{lh // 2}"),
               writes=(sres,))
            act(le, sbank[0:64, 0:512], AF.Exp, (sres,), (ler,), scale=0.125)
            tt(le, le, masklat[:, nc4 * 512:(nc4 + 1) * 512], ALU.mult,
               (ler, "c.masklat"), (ler,))
            mm(pU[0:65, 0:512], rv_l[:, lh * 65:lh * 65 + 65], le,
               start=True, stop=True,
               reads=(ler, f"rvl.{lh}", "rvl.ones"), writes=(bnU,))
            act(lat_row[0:1, 0:512], pU[64:65, 0:512], AF.Ln,
                (bnU, "c.cv1"), ("lrow.0",), bias=cvec[0:1, 1:2])
            act(lat_row[0:1, 0:512], lat_row[0:1, 0:512], AF.Exp,
                ("lrow.0",), ("lrow.0",), scale=-1.0)
            cp_v(lat_rbf[0:1, 0:512], lat_row[0:1, 0:512], ("lrow.0",),
                 ("lrbf",))
            mm(pB[0:64, 0:512], ones_r[0:1, 0:64], lat_rbf[0:1, 0:512],
               start=True, stop=True, reads=("lrbf", "c.ones"),
               writes=(bnB,))
            cp_s(lat_u[:, 0:512], pU[0:64, 0:512], (bnU,), ("lat_u",))
            P.op("vector", lambda e,
                 o=latout_fm[pb:pb + 64,
                             (lh // 2) * 2048 + nc4 * 512:
                             (lh // 2) * 2048 + (nc4 + 1) * 512],
                 i=lat_u[:, 0:512], r=pB[0:64, 0:512]:
                 e.tensor_tensor(o, i, r, ALU.mult),
                 reads=("lat_u", bnB), writes=(f"latout.{lh}.{nc4}",))
        if nc4 + 2 < 4:
            rq_emit(nc4 + 2)

    for b in range(NQB):
        qblock(b)

    P.barrier()
    esB.close()

    esA = ExitStack()

    def sbA(name, shape, dt, **kw):
        return esA.enter_context(nc.sbuf_tensor(name, list(shape), dt, **kw))

    wgT = sbA("wgT", [128, 8 * FFI], BF16)
    wuT = sbA("wuT", [128, 8 * FFI], BF16)
    wdT = sbA("wdT", [128, 22 * 1024], BF16)
    xq_t = [sbA(f"xq_t{i}", [128, 1024], BF16) for i in range(2)]
    h_tm = [sbA(f"h_tm{i}", [128, 1024], F32) for i in range(4)]
    h_q = [sbA(f"h_q{i}", [128, 1024], BF16) for i in range(2)]
    hqT = [sbA(f"hqT{i}", [128, 1024], BF16) for i in range(2)]
    y1_in = [sbA(f"y1_in{i}", [128, 1024], F32) for i in range(2)]
    y_out = [sbA(f"y_out{i}", [128, 1024], F32) for i in range(2)]

    for t0 in range(2):
        P.op("sync", lambda e, o=y1_in[t0][:],
             i=y_d[t0 * 128:(t0 + 1) * 128, :]: e.dma_start(out=o, in_=i),
             reads=(f"y1d.{t0}",), writes=(f"y1_in{t0}",),
             sem=f"d_ya{t0}", inc=16)
        P.op("sync", lambda e, o=xq_t[t0][:],
             i=xq_d[:, t0 * 1024:(t0 + 1) * 1024]: e.dma_start(out=o, in_=i),
             reads=(f"xqd.{t0}",), writes=(f"xq_t{t0}",),
             sem=f"d_xa{t0}", inc=16)
    for r in range(NCORES):
        g = r // 3
        P.op("sync", lambda e,
             o=wgT[:].rearrange("p (kc n) -> p kc n", kc=8)
             [:, :, r * 352:(r + 1) * 352],
             i=wgu_g[r * 2048:r * 2048 + 1024, :].rearrange(
                 "(kc p) n -> p kc n", p=128):
             e.dma_start(out=o, in_=i),
             reads=("wgu_g",), writes=(f"wgT.{r}",),
             sem=f"d_wg{g}", inc=16, serial=False)
        P.op("gpsimd", lambda e,
             o=wuT[:].rearrange("p (kc n) -> p kc n", kc=8)
             [:, :, r * 352:(r + 1) * 352],
             i=wgu_g[r * 2048 + 1024:r * 2048 + 2048, :].rearrange(
                 "(kc p) n -> p kc n", p=128):
             e.dma_start(out=o, in_=i),
             reads=("wgu_g",), writes=(f"wuT.{r}",),
             sem=f"d_wu{g}", inc=16, serial=False)
        P.op("scalar", lambda e,
             o=wdT[:].rearrange("p (kc n) -> p kc n", kc=22)
             [:, :, r * 128:(r + 1) * 128],
             i=wd_g[r * FFI:(r + 1) * FFI, :].rearrange(
                 "(kc p) n -> p kc n", p=128):
             e.dma_start(out=o, in_=i),
             reads=("wd_g",), writes=(f"wdT.{r}",),
             sem="d_wd", inc=16, serial=False)

    GLAST = (2, 5, 7)

    def rtok(pfx, n0, nw):
        g = ((n0 + nw - 1) // 352) // 3
        return tuple(f"{pfx}.{GLAST[gi]}" for gi in range(g + 1))

    SLICES = ((0, 1024), (1024, 1024), (2048, 768))
    slices = [(t, si, c0, w) for t in range(NCH)
              for si, (c0, w) in enumerate(SLICES)]

    def a_stage1a(i):
        t, si, c0, w = slices[i]
        tp_ = t % 2
        if si == 0 and t >= 2:
            P.op("sync", lambda e, o=y1_in[tp_][:],
                 i=y_d[t * 128:(t + 1) * 128, :]: e.dma_start(out=o, in_=i),
                 reads=(f"y1d.{t}",), writes=(f"y1_in{tp_}",),
                 sem=f"d_ya{tp_}", inc=16)
            P.op("sync", lambda e, o=xq_t[tp_][:],
                 i=xq_d[:, t * 1024:(t + 1) * 1024]: e.dma_start(out=o, in_=i),
                 reads=(f"xqd.{t}",), writes=(f"xq_t{tp_}",),
                 sem=f"d_xa{tp_}", inc=16)
        hp = i % 4
        for wT, pb, res0, res1, rr in ((wgT, psE, "E0", "E1", "wgT"),
                                       (wuT, psO, "O0", "O1", "wuT")):
            for kc in range(8):
                mm(pb[0:128, 0:512],
                   xq_t[tp_][:, kc * 128:(kc + 1) * 128],
                   wT[:, kc * FFI + c0:kc * FFI + c0 + 512],
                   start=(kc == 0), stop=(kc == 7),
                   reads=rtok(rr, c0, 512) + (f"xq_t{tp_}",),
                   writes=(res0,))
                mm(pb[0:128, 512:w],
                   xq_t[tp_][:, kc * 128:(kc + 1) * 128],
                   wT[:, kc * FFI + c0 + 512:kc * FFI + c0 + w],
                   start=(kc == 0), stop=(kc == 7),
                   reads=rtok(rr, c0, w) + (f"xq_t{tp_}",),
                   writes=(res1,))
        act(h_tm[hp][:, 0:w], psE[0:128, 0:w], AF.Silu, ("E0", "E1"),
            (f"hs{hp}",))

    def a_stage1b(i):
        t, si, c0, w = slices[i]
        hp = i % 4
        P.op("vector", lambda e, o=h_tm[hp][:, 0:w],
             i=h_tm[hp][:, 0:w], r=psO[0:128, 0:w]:
             e.tensor_tensor(o, i, r, ALU.mult),
             reads=(f"hs{hp}", "O0", "O1"),
             writes=(f"h{hp}.0", f"h{hp}.512"))

    def a_quant(i, half):
        t, si, c0, w = slices[i]
        par = i % 2
        hp = i % 4
        h0 = half * 512
        hw = 512 if half == 0 else w - 512
        quant(h_tm[hp][:, h0:h0 + hw], h_q[par][:, h0:h0 + hw], 128, hw,
              (f"h{hp}.{h0}",), (f"hq{par}",))

    def a_stage2(i):
        t, si, c0, w = slices[i]
        tp_ = t % 2
        par = i % 2
        nkc = w // 128
        for grp in range((nkc + 3) // 4):
            j0 = grp * 4
            j1 = min(j0 + 4, nkc)
            sl = grp % 2
            for j in range(j0, j1):
                trans(ptrF32[0:128, sl * 512 + (j - j0) * 128:
                             sl * 512 + (j - j0 + 1) * 128],
                      h_q[par][:, j * 128:(j + 1) * 128], ident[:],
                      reads=(f"hq{par}", "c.ident"), writes=(f"ptr.s{sl}",))
            cp_s(hqT[par][:, j0 * 128:j1 * 128],
                 ptrF32[0:128, sl * 512:sl * 512 + (j1 - j0) * 128],
                 (f"ptr.s{sl}",), (f"hqT{par}",))
        for j in range(nkc):
            kc = c0 // 128 + j
            for nt in range(2):
                bank, bres = (p4, "p4") if nt == 0 else (p5, "p5")
                mm(bank[:, 0:512],
                   hqT[par][:, j * 128:(j + 1) * 128],
                   wdT[:, kc * 1024 + nt * 512:kc * 1024 + (nt + 1) * 512],
                   start=(kc == 0), stop=(kc == 21),
                   reads=("wdT.7", f"hqT{par}"), writes=(bres,))
        if si == 2:
            for nt in range(2):
                bank, bres = (p4, "p4") if nt == 0 else (p5, "p5")
                P.op("vector", lambda e,
                     o=y_out[tp_][:, nt * 512:(nt + 1) * 512],
                     i=bank[:, 0:512],
                     r=y1_in[tp_][:, nt * 512:(nt + 1) * 512]:
                     e.tensor_tensor(o, i, r, ALU.add),
                     reads=(bres, f"y1_in{tp_}"), writes=(f"yo{tp_}.{nt}",))
            P.op("scalar", lambda e, o=y_d[t * 128:(t + 1) * 128, :],
                 i=y_out[tp_][:]: e.dma_start(out=o, in_=i),
                 reads=(f"yo{tp_}.0", f"yo{tp_}.1"), writes=(f"yfinal.{t}",),
                 sem=f"d_yf{tp_}", inc=16)

    for i in range(len(slices) + 2):
        if 1 <= i <= len(slices):
            a_quant(i - 1, 0)
        if i < len(slices):
            a_stage1a(i)
        if 1 <= i <= len(slices):
            a_quant(i - 1, 1)
        if i < len(slices):
            a_stage1b(i)
        if i >= 2:
            a_stage2(i - 2)

    P.barrier()
    esA.close()

    sems = {}
    sem_stack = ExitStack()
    for name in sorted(P.sems):
        sems[name] = sem_stack.enter_context(nc.semaphore(name))
    with nc.Block() as block:
        P.emit_all(block, sems)
    sem_stack.close()
    es.close()
    return nc


_NC_CACHE = {}


def kernel(L0_attn_x, L0_attn_qkv_w, L0_attn_qkv_b, L0_attn_rq_w, L0_attn_rq_b,
           L0_attn_rkv_w, L0_attn_rkv_b, L0_attn_rout_w, L0_attn_rout_b,
           L0_attn_out_w, L0_attn_out_b, L0_attn_gl_w, L0_attn_gr_w,
           L0_ffn_w_gate, L0_ffn_w_up, L0_ffn_w_down):
    f32 = np.float32
    x = np.ascontiguousarray(np.asarray(L0_attn_x, f32).reshape(S, D))
    qkv_w = np.ascontiguousarray(np.asarray(L0_attn_qkv_w, f32))
    qkv_b = np.asarray(L0_attn_qkv_b, f32)
    rq_w = np.ascontiguousarray(np.asarray(L0_attn_rq_w, f32))
    rq_b = np.asarray(L0_attn_rq_b, f32)
    rkv_w = np.ascontiguousarray(np.asarray(L0_attn_rkv_w, f32))
    rkv_b = np.asarray(L0_attn_rkv_b, f32)
    rout_w = np.ascontiguousarray(np.asarray(L0_attn_rout_w, f32))
    rout_b = np.asarray(L0_attn_rout_b, f32)
    out_w = np.ascontiguousarray(np.asarray(L0_attn_out_w, f32))
    out_b = np.asarray(L0_attn_out_b, f32)
    gl_w = np.asarray(L0_attn_gl_w, f32)
    gr_w = np.asarray(L0_attn_gr_w, f32)
    w_gate = np.ascontiguousarray(np.asarray(L0_ffn_w_gate, f32))
    w_up = np.ascontiguousarray(np.asarray(L0_ffn_w_up, f32))
    w_down = np.ascontiguousarray(np.asarray(L0_ffn_w_down, f32))

    def fm(vec, cols):
        return np.ascontiguousarray(vec.reshape(cols, 128).T)

    gl_col = np.ascontiguousarray(np.concatenate([gl_w[:, 0], gl_w[:, 0]]).reshape(128, 1))
    gr2 = np.zeros((128, 2), f32)
    gr2[0:64, 0] = gr_w[:, 0]
    gr2[64:128, 1] = gr_w[:, 0]
    ident = np.eye(128, dtype=BFnp)
    maskg = make_local_mask(False)
    mask0c0 = make_local_mask(True)

    nc = _NC_CACHE.get("nc")
    if nc is None:
        nc = build_program()
        _NC_CACHE["nc"] = nc
    in_maps = []
    for c in range(NCORES):
        lo = c * T - 128
        xh = np.zeros((NHC * 128, D), f32)
        if lo < 0:
            xh[128:] = x[0:T]
        else:
            xh[:] = x[lo:(c + 1) * T]
        in_maps.append({
            "x": xh,
            "qkv_w": qkv_w, "rq_w": rq_w, "rkv_w": rkv_w,
            "rout_w": rout_w, "out_w": out_w,
            "gl_col": gl_col, "gr2": gr2,
            "qb_fm": fm(qkv_b[0:1024], 8), "kb_fm": fm(qkv_b[1024:2048], 8),
            "vb_row": np.ascontiguousarray(qkv_b[2048:3072].reshape(1, D)),
            "rqb_fm": fm(rq_b, 2), "rkvb_fm": fm(rkv_b, 4),
            "routb_fm": fm(rout_b, 8),
            "outb_row": np.ascontiguousarray(out_b.reshape(1, D)),
            "wg_sh": np.ascontiguousarray(w_gate[c * 352:(c + 1) * 352]),
            "wu_sh": np.ascontiguousarray(w_up[c * 352:(c + 1) * 352]),
            "wd_sh": np.ascontiguousarray(w_down[c * 128:(c + 1) * 128]),
            "mask0": mask0c0 if c == 0 else maskg,
            "maskg": maskg,
            "mask_lat": make_lat_mask(c),
            "ident": ident,
        })
    res = run_bass_kernel_spmd(nc, in_maps, core_ids=list(range(NCORES)))
    y = np.concatenate([np.asarray(r["y"], np.float32)
                        for r in res.results], axis=0)
    return y.reshape(1, S, D)


# version 38 (final)
# speedup vs baseline: 1.0303x; 1.0071x over previous
import numpy as np
import ml_dtypes
import concourse.bass as bass
import concourse.mybir as mybir
import concourse.bass_utils as _bass_utils
from concourse.bass_utils import run_bass_kernel_spmd

_orig_run_command = _bass_utils.run_command


def _run_command_ldwopt(cmd, *a, **kw):
    if isinstance(cmd, list):
        cmd = ["--enable-ldw-opt=true" if c == "--enable-ldw-opt=false" else c
               for c in cmd]
    return _orig_run_command(cmd, *a, **kw)


_bass_utils.run_command = _run_command_ldwopt
from contextlib import ExitStack

F32 = mybir.dt.float32
BF16 = mybir.dt.bfloat16
I32 = mybir.dt.int32
ALU = mybir.AluOpType
AF = mybir.ActivationFunctionType
AX = mybir.AxisListType
BFnp = ml_dtypes.bfloat16

NCORES = 8
S, D = 16384, 1024
T = S // NCORES
NCH = T // 128
NHC = NCH + 1
NQB = T // 256
NH = 16
LH = 4
FFI = 2816
MAGIC = float(np.float32(3 * 2 ** 21))
C_EXP = 0x7F800000
C_RP = 0x7F000000


class Plan:
    ENGINES = ("sync", "scalar", "vector", "tensor", "gpsimd")

    def __init__(self):
        self.ops = []
        self.cnt = {}
        self.res = {}
        self.sems = set()

    def _sem(self, name):
        if name not in self.cnt:
            self.cnt[name] = 0
            self.sems.add(name)
        return name

    def op(self, eng, emit, reads=(), writes=(), sem=None, inc=1,
           serial=False):
        sem = self._sem(sem or ("s_" + eng))
        waits = {}

        def need(psem, v, peng):
            if v <= 0:
                return
            if peng == "tensor" and eng == "tensor":
                return
            waits[psem] = max(waits.get(psem, 0), v)

        for r in reads:
            rec = self.res.get(r)
            if rec and rec["w"]:
                need(*rec["w"])
        for w in writes:
            rec = self.res.get(w)
            if rec:
                if rec["w"]:
                    need(*rec["w"])
                for dep in rec["r"]:
                    need(*dep)
        self.cnt[sem] += inc
        me = (sem, self.cnt[sem], eng)
        for r in reads:
            self.res.setdefault(r, {"w": None, "r": []})["r"].append(me)
        for w in writes:
            self.res[w] = {"w": me, "r": []}
        post_self = (sem, self.cnt[sem]) if serial else None
        self.ops.append((eng, emit, waits, sem, inc, post_self))
        return me

    def rebind(self, resources, token):
        w = self.res[token]["w"]
        for r in resources:
            self.res[r] = {"w": w, "r": []}

    def fence(self, eng, sem, name):
        sem = self._sem(sem)
        fs = self._sem("f_" + name)
        cur = self.cnt[sem]
        self.cnt[fs] += 1
        fv = self.cnt[fs]

        def emit(e, sems):
            e.wait_ge(sems[sem], cur)
            e.sem_inc(sems[fs], 1)

        tok = "tok_" + name + str(fv)
        self.res[tok] = {"w": (fs, fv, eng), "r": []}
        self.ops.append((eng, ("RAW", emit), {}, None, 0, None))
        return tok

    def barrier(self):
        snap = dict(self.cnt)
        for eng in self.ENGINES:
            def emit(e, sems, snap=snap):
                for s_, v in sorted(snap.items()):
                    if v > 0:
                        e.wait_ge(sems[s_], v)
            self.ops.append((eng, ("RAW", emit), {}, None, 0, None))

    def emit_all(self, block, sems):
        per = {e: [] for e in self.ENGINES}
        for rec in self.ops:
            per[rec[0]].append(rec)

        def run_engine(e_obj, eng):
            hi = {}
            for (_, emit, waits, sem, inc, post_self) in per[eng]:
                if isinstance(emit, tuple) and emit[0] == "RAW":
                    emit[1](e_obj, sems)
                    continue
                for wsem, wval in sorted(waits.items()):
                    if hi.get(wsem, 0) < wval:
                        e_obj.wait_ge(sems[wsem], wval)
                        hi[wsem] = wval
                ins = emit(e_obj)
                if sem is not None:
                    ins.then_inc(sems[sem], inc)
                if post_self is not None:
                    e_obj.wait_ge(sems[post_self[0]], post_self[1])
                    hi[post_self[0]] = max(hi.get(post_self[0], 0),
                                           post_self[1])

        block.sync(lambda e: run_engine(e, "sync"))
        block.scalar(lambda e: run_engine(e, "scalar"))
        block.vector(lambda e: run_engine(e, "vector"))
        block.tensor(lambda e: run_engine(e, "tensor"))
        block.gpsimd(lambda e: run_engine(e, "gpsimd"))


def make_local_mask(first_block_core0):
    kk = np.arange(384)[:, None]
    qi = np.arange(256)[None, :]
    m = (kk >= qi) & (kk <= qi + 127) & (kk >= 1)
    if first_block_core0:
        m &= kk >= 128
    out = np.zeros((128, 768), dtype=BFnp)
    for c3 in range(3):
        out[:, c3 * 256:(c3 + 1) * 256] = m[c3 * 128:(c3 + 1) * 128].astype(BFnp)
    return out


def make_lat_mask(core):
    tok = np.arange(T) + core * T
    cid = tok // 256
    return (np.arange(64)[:, None] < cid[None, :]).astype(BFnp)


def build_program():
    nc = bass.Bass("TRN2", target_bir_lowering=False)
    P = Plan()

    def din(name, shape, dt):
        return nc.dram_tensor(name, list(shape), dt, kind="ExternalInput").ap()

    x_d = din("x", [NHC * 128, D], F32)
    qkvw_d = din("qkv_w", [D, 3 * D], F32)
    rqw_d = din("rq_w", [D, 256], F32)
    rkvw_d = din("rkv_w", [D, 512], F32)
    routw_d = din("rout_w", [256, D], F32)
    outw_d = din("out_w", [D, D], F32)
    glc_d = din("gl_col", [128, 1], F32)
    grc_d = din("gr2", [128, 2], F32)
    qb_d = din("qb_fm", [128, 8], F32)
    kb_d = din("kb_fm", [128, 8], F32)
    vb_d = din("vb_row", [1, D], F32)
    rqb_d = din("rqb_fm", [128, 2], F32)
    rkvb_d = din("rkvb_fm", [128, 4], F32)
    routb_d = din("routb_fm", [128, 8], F32)
    outb_d = din("outb_row", [1, D], F32)
    wg_sh_d = din("wg_sh", [352, D], F32)
    wu_sh_d = din("wu_sh", [352, D], F32)
    wd_sh_d = din("wd_sh", [128, FFI], F32)
    mask0_d = din("mask0", [128, 768], BF16)
    maskg_d = din("maskg", [128, 768], BF16)
    masklat_d = din("mask_lat", [64, T], BF16)
    ident_d = din("ident", [128, 128], BF16)

    y_d = nc.dram_tensor("y", [T, D], F32, kind="ExternalOutput").ap()
    xq_d = nc.dram_tensor("xq_scratch", [128, NCH * 1024], BF16,
                          kind="Internal").ap()
    wgu_c = nc.dram_tensor("wgu_c", [2048, 352], BF16, kind="Internal").ap()
    wgu_g = nc.dram_tensor("wgu_g", [2048 * NCORES, 352], BF16,
                           kind="Internal").ap()
    wd_c = nc.dram_tensor("wd_c", [FFI, 128], BF16, kind="Internal").ap()
    wd_g = nc.dram_tensor("wd_g", [FFI * NCORES, 128], BF16,
                          kind="Internal").ap()
    rkv_c = nc.dram_tensor("rkv_c", [8, 512], BF16, kind="Internal").ap()
    rkv_g = nc.dram_tensor("rkv_g", [64, 512], BF16, kind="Internal").ap()
    RG = [list(range(NCORES))]

    es = ExitStack()

    def sb(name, shape, dt, **kw):
        return es.enter_context(nc.sbuf_tensor(name, list(shape), dt, **kw))

    ident = sb("identsb", [128, 128], BF16)
    cvec = sb("cvec", [128, 3], F32)
    ones_r = sb("ones_r", [1, 128], BF16)
    pool = [sb(f"qp{i}", [128, 1024], F32) for i in range(3)]
    sc_a = sb("qsA", [128, 176], F32)
    sc_s = sb("qsS", [128, 176], F32)
    sc_r = sb("qsR", [128, 176], F32)
    gl_col = sb("glcolsb", [128, 1], BF16)
    gr2 = sb("gr2sb", [128, 2], BF16)
    qb_fm = sb("qbsb", [128, 8], F32)
    kb_fm = sb("kbsb", [128, 8], F32)
    vb_row = sb("vbsb", [1, D], BF16)
    rqb_fm = sb("rqbsb", [128, 2], F32)
    rkvb_fm = sb("rkvbsb", [128, 4], F32)
    routb_fm = sb("routbsb", [128, 8], F32)
    outb_row = sb("outbsb", [1, D], BF16)

    psE = es.enter_context(nc.psum_tensor("psE", [128, 1024], F32))
    psO = es.enter_context(nc.psum_tensor("psO", [128, 1024], F32))
    p4 = es.enter_context(nc.psum_tensor("p4", [128, 512], F32))
    p5 = es.enter_context(nc.psum_tensor("p5", [128, 512], F32))
    ptr = es.enter_context(nc.psum_tensor("ptr", [128, 2048], BF16))

    def dma(q, out, in_, reads, writes, serial=True):
        P.op(q, lambda e, o=out, i=in_: e.dma_start(out=o, in_=i),
             reads=reads, writes=writes, sem="d_" + q, inc=16, serial=serial)

    def cp_s(out, in_, reads, writes):
        P.op("scalar", lambda e, o=out, i=in_: e.copy(o, i),
             reads=reads, writes=writes)

    def cp_v(out, in_, reads, writes):
        P.op("vector", lambda e, o=out, i=in_: e.tensor_copy(o, i),
             reads=reads, writes=writes)

    def act(out, in_, func, reads, writes, bias=0.0, scale=1.0):
        P.op("scalar",
             lambda e, o=out, i=in_, f=func, b=bias, s=scale:
             e.activation(o, i, f, bias=b, scale=s),
             reads=reads, writes=writes)

    def tt(out, a, b, op, reads, writes):
        P.op("vector",
             lambda e, o=out, x=a, y=b, p=op: e.tensor_tensor(o, x, y, p),
             reads=reads, writes=writes)

    def mm(out, lhsT, rhs, start, stop, reads, writes, tp=None):
        P.op("tensor",
             lambda e, o=out, l=lhsT, r=rhs, st=start, sp=stop, t=tp:
             e.matmul(o, l, r, start=st, stop=sp, tile_position=t),
             reads=reads, writes=writes)

    def trans(out, in_, idn, reads, writes):
        P.op("tensor",
             lambda e, o=out, i=in_, d=idn:
             e.matmul(o, i, d, start=True, stop=True),
             reads=reads, writes=writes)

    def memset(ap_, val, writes, reads=()):
        P.op("vector", lambda e, a=ap_, v=val: e.memset(a, v), reads=reads,
             writes=writes)

    def quant(src, dst, p, w, reads, writes):
        nb = w // 16
        b0, b1, b2 = (x[0:p, 0:w] for x in pool)
        A, Ssc, R = sc_a[0:p, 0:nb], sc_s[0:p, 0:nb], sc_r[0:p, 0:nb]
        r_src = tuple(reads)
        P.op("vector", lambda e, o=A, i=src: e.tensor_reduce(
            o, i.rearrange("p (b w) -> p b w", w=16), AX.X, ALU.max,
            apply_absolute_value=True), reads=r_src, writes=("qsA",))
        P.op("vector", lambda e, o=Ssc, i=A: e.tensor_scalar(
            o, i, float(np.float32(1.0) / np.float32(6.0)), 1e-12,
            ALU.mult, ALU.max),
            reads=("qsA",), writes=("qsS",))
        P.op("vector", lambda e, o=R, i=Ssc: e.reciprocal(o, i),
             reads=("qsS",), writes=("qsR",))
        rbc = R.unsqueeze(2).broadcast_to([p, nb, 16])
        sbc = Ssc.unsqueeze(2).broadcast_to([p, nb, 16])
        v3 = src.rearrange("p (b w) -> p b w", w=16)
        P.op("vector", lambda e, o=b0.rearrange("p (b w) -> p b w", w=16),
             i=v3, r=rbc: e.tensor_tensor(o, i, r, ALU.mult),
             reads=r_src + ("qsR",), writes=("qp0",))
        P.op("vector", lambda e, o=b1.bitcast(I32), i=b0.bitcast(I32):
             e.tensor_scalar(o, i, C_EXP, None, ALU.bitwise_and),
             reads=("qp0",), writes=("qp1",))
        P.op("vector", lambda e, o=b1, i=b1: e.tensor_scalar(
            o, i, 1.0, 4.0, ALU.max, ALU.min),
            reads=("qp1",), writes=("qp1",))
        P.op("vector", lambda e, o=b2.bitcast(I32), i=b1.bitcast(I32):
             e.tensor_scalar(o, i, C_RP, -1, ALU.subtract, ALU.mult),
             reads=("qp1",), writes=("qp2",))
        P.op("vector", lambda e, o=b2, i=b0, r=b2: e.tensor_tensor(
            o, i, r, ALU.mult), reads=("qp0", "qp2"), writes=("qp2",))
        P.op("vector", lambda e, o=b0, i=b2: e.tensor_scalar(
            o, i, MAGIC, MAGIC, ALU.add, ALU.subtract),
            reads=("qp2",), writes=("qp0",))
        P.op("vector", lambda e, o=b2, i=b0, r=b1: e.tensor_tensor(
            o, i, r, ALU.mult), reads=("qp0", "qp1"), writes=("qp2",))
        P.op("vector", lambda e, o=dst.rearrange("p (b w) -> p b w", w=16),
             i=b2.rearrange("p (b w) -> p b w", w=16), r=sbc:
             e.tensor_tensor(o, i, r, ALU.mult),
             reads=("qp2", "qsS"), writes=tuple(writes))

    dma("sync", ident[:], ident_d, (), ("c.ident",))
    memset(ones_r[:], 1.0, ("c.ones",))
    memset(cvec[:, 0:1], 1e-5, ("c.cv0",))
    memset(cvec[:, 1:2], 1e-30, ("c.cv1",))
    memset(cvec[:, 2:3], 1.0, ("c.cv2",))
    dma("gpsimd", gl_col[:], glc_d, (), ("c.gl",))
    dma("gpsimd", gr2[:], grc_d, (), ("c.gr",))
    dma("sync", qb_fm[:], qb_d, (), ("c.qb",))
    dma("sync", kb_fm[:], kb_d, (), ("c.kb",))
    dma("gpsimd", vb_row[:], vb_d, (), ("c.vb",))
    dma("sync", rqb_fm[:], rqb_d, (), ("c.rqb",))
    dma("sync", rkvb_fm[:], rkvb_d, (), ("c.rkvb",))
    dma("sync", routb_fm[:], routb_d, (), ("c.routb",))
    dma("gpsimd", outb_row[:], outb_d, (), ("c.outb",))

    esB = ExitStack()

    def sbB(name, shape, dt, **kw):
        return esB.enter_context(nc.sbuf_tensor(name, list(shape), dt, **kw))

    x_roll = sbB("x_roll", [128, 2 * 1024], F32)
    x_res = sbB("x_res", [128, 1024], F32)
    xn_fm = sbB("xn_fm", [128, 8 * NHC * 128], BF16)
    xn_tm = [x_roll[:, 0:1024], x_roll[:, 1024:2048]]
    xn_bf = [sbB(f"xn_bf{i}", [128, 1024], BF16) for i in range(2)]
    xq_tm = xn_bf
    mv = [sbB(f"mv{i}", [128, 2], F32) for i in range(2)]
    bnst = [sbB(f"bnst{i}", [128, 12], F32) for i in range(2)]
    rstd = [sbB(f"rstd{i}", [128, 1], F32) for i in range(2)]
    masks = sbB("masks", [128, 2 * 768], BF16)
    masklat = sbB("masklat", [64, T], BF16)
    pooled_bf = sbB("pooled_bf", [128, 64], BF16)
    rk_fm = sbB("rk_fm", [128, 128], BF16)
    rv_l = sbB("rv_l", [64, 4 * 65], BF16)
    rkv_bf = sbB("rkv_bf", [128, 32], BF16)
    xqT_st = sbB("xqT_st", [128, 1024], BF16)

    dma("sync", masks[:, 0:768], mask0_d, (), ("c.mask0",))
    dma("sync", masks[:, 768:1536], maskg_d, (), ("c.maskg",))
    dma("sync", masklat[:], masklat_d, (), ("c.masklat",))

    esSh = ExitStack()
    w_st = esSh.enter_context(nc.sbuf_tensor("w_st", [128, 1024], F32))
    shq = esSh.enter_context(nc.sbuf_tensor("shq", [128, 1024], BF16))
    trst = esSh.enter_context(nc.sbuf_tensor("trst", [128, 1024], BF16))

    ptrF32 = ptr[:].bitcast(F32)

    def PTR(sl, w=128, p=128):
        s = (sl % 2) * 512
        return ptrF32[0:p, s:s + w]

    for wsrc, drow0 in ((wg_sh_d, 0), (wu_sh_d, 1024)):
        for rt, rows_n in enumerate((128, 128, 96)):
            dma("sync", w_st[0:rows_n, :], wsrc[rt * 128:rt * 128 + rows_n, :],
                (), ("w_st",))
            quant(w_st[0:rows_n, :], shq[0:rows_n, :], rows_n, 1024,
                  ("w_st",), ("shq",))
            for grp in range(2):
                j0, sl = grp * 4, grp
                for j in range(4):
                    kc = j0 + j
                    trans(ptrF32[0:128, sl * 512 + j * rows_n:
                                 sl * 512 + (j + 1) * rows_n],
                          shq[0:rows_n, kc * 128:(kc + 1) * 128],
                          ident[0:rows_n, 0:rows_n],
                          reads=("shq", "c.ident"), writes=(f"ptr.s{sl}",))
                cp_s(trst[:, sl * 512:sl * 512 + 4 * rows_n],
                     ptrF32[0:128, sl * 512:sl * 512 + 4 * rows_n],
                     (f"ptr.s{sl}",), (f"trst.s{sl}",))
                dma("scalar",
                    wgu_c[drow0 + j0 * 128:drow0 + (j0 + 4) * 128,
                          rt * 128:rt * 128 + rows_n]
                    .rearrange("(kc p) n -> p kc n", p=128),
                    trst[:, sl * 512:sl * 512 + 4 * rows_n]
                    .rearrange("p (kc n) -> p kc n", n=rows_n),
                    (f"trst.s{sl}",),
                    (f"wguc.{drow0}.{rt}.{j0}",))
    for half, (hc0, w) in enumerate(((0, 1024), (1024, 1024), (2048, 768))):
        dma("sync", w_st[:, 0:w], wd_sh_d[:, hc0:hc0 + w], (), ("w_st",))
        quant(w_st[:, 0:w], shq[:, 0:w], 128, w, ("w_st",), ("shq",))
        nj = w // 128
        for grp in range((nj + 3) // 4):
            j0 = grp * 4
            j1 = min(j0 + 4, nj)
            nb = j1 - j0
            sl = grp % 2
            for j in range(j0, j1):
                trans(ptrF32[0:128, sl * 512 + (j - j0) * 128:
                             sl * 512 + (j - j0 + 1) * 128],
                      shq[:, j * 128:(j + 1) * 128], ident[:],
                      reads=("shq", "c.ident"), writes=(f"ptr.s{sl}",))
            cp_s(trst[:, sl * 512:sl * 512 + nb * 128],
                 ptrF32[0:128, sl * 512:sl * 512 + nb * 128],
                 (f"ptr.s{sl}",), (f"trst.s{sl}",))
            kc0 = hc0 // 128 + j0
            dma("scalar",
                wd_c[kc0 * 128:(kc0 + nb) * 128, :]
                .rearrange("(kc p) n -> p kc n", p=128),
                trst[:, sl * 512:sl * 512 + nb * 128]
                .rearrange("p (kc n) -> p kc n", n=128),
                (f"trst.s{sl}",), (f"wdc.{kc0}",))
    shard_tok = P.fence("scalar", "d_scalar", "shardw")
    P.op("gpsimd", lambda e: e.collective_compute(
        "AllGather", ALU.bypass, replica_groups=RG,
        ins=[wgu_c.opt()], outs=[wgu_g.opt()]),
        reads=(shard_tok,), writes=("wgu_g",), sem="col_wgu", inc=1)
    P.op("gpsimd", lambda e: e.collective_compute(
        "AllGather", ALU.bypass, replica_groups=RG,
        ins=[wd_c.opt()], outs=[wd_g.opt()]),
        reads=(shard_tok,), writes=("wd_g",), sem="col_wd", inc=1)
    esSh.close()

    qkv_fm = sbB("qkv_fm", [128, 8 * 3072], BF16)
    rqw_fm = sbB("rqw_fm", [128, 8 * 256], BF16)
    routw_fm = sbB("routw_fm", [128, 2 * 1024], BF16)
    outw_fm = sbB("outw_fm", [128, 8 * 1024], BF16)
    esW = ExitStack()
    rkvw_fm = esW.enter_context(
        nc.sbuf_tensor("rkvw_fm", [128, 8 * 512], BF16))
    contrib = esW.enter_context(nc.sbuf_tensor("contrib", [8, 512], BF16))
    gath = esW.enter_context(nc.sbuf_tensor("gath", [64, 512], BF16))

    for dst, srcw, kcn, rname in (
            (qkv_fm, qkvw_d, 8, "w.qkv"), (rqw_fm, rqw_d, 8, "w.rq"),
            (rkvw_fm, rkvw_d, 8, "w.rkv"), (routw_fm, routw_d, 2, "w.rout"),
            (outw_fm, outw_d, 8, "w.out")):
        dma("gpsimd", dst[:].rearrange("p (kc n) -> p kc n", kc=kcn),
            srcw.rearrange("(kc p) n -> p kc n", p=128), (), (rname,),
            serial=False)
    wtok = P.fence("gpsimd", "d_gpsimd", "wts")
    P.rebind(("w.qkv", "w.rq", "w.rkv", "w.rout", "w.out"), wtok)

    def xcol(kc, hc):
        return kc * (NHC * 128) + hc * 128

    def c_stage1(hc):
        par = hc % 2
        if hc == 0:
            P.op("sync", lambda e, o=x_roll[:, 0:1024], i=x_d[0:128, :]:
                 e.dma_start(out=o, in_=i), reads=(), writes=("xroll.0",),
                 sem="d_x0", inc=16)
        if hc + 1 < NHC:
            npar = (hc + 1) % 2
            P.op("sync", lambda e,
                 o=x_roll[:, npar * 1024:(npar + 1) * 1024],
                 i=x_d[(hc + 1) * 128:(hc + 2) * 128, :]:
                 e.dma_start(out=o, in_=i), reads=(),
                 writes=(f"xroll.{npar}",), sem=f"d_x{npar}", inc=16)
        xr = x_roll[:, par * 1024:(par + 1) * 1024]
        rx = f"xroll.{par}"
        P.op("vector", lambda e, o=bnst[par][:, 0:6], i=xr[:, 0:512]:
             e.bn_stats(o, i), reads=(rx,), writes=(f"bn0.{par}",))
        P.op("vector", lambda e, o=bnst[par][:, 6:12], i=xr[:, 512:1024]:
             e.bn_stats(o, i), reads=(rx,), writes=(f"bn1.{par}",))
        P.op("vector", lambda e, o=mv[par][:], i=bnst[par][:]:
             e.bn_aggr(o, i),
             reads=(f"bn0.{par}", f"bn1.{par}"), writes=(f"mv.{par}",))
        act(rstd[par][:], mv[par][:, 1:2], AF.Ln,
            (f"mv.{par}", "c.cv0"), (f"rstd.{par}",), bias=cvec[:, 0:1])
        act(rstd[par][:], rstd[par][:], AF.Exp, (f"rstd.{par}",),
            (f"rstd.{par}",), scale=-0.5)
        P.op("vector", lambda e, o=xn_tm[par], i=xr, m=mv[par][:, 0:1],
             r=rstd[par][:]:
             e.tensor_scalar(o, i, m, r, ALU.subtract, ALU.mult),
             reads=(rx, f"mv.{par}", f"rstd.{par}"), writes=(rx,))
        cp_s(xn_bf[par][:], xn_tm[par], (rx,),
             (f"xn_bf{par}",))
        for grp in range(2):
            j0, sl = grp * 4, grp
            for j in range(4):
                kc = j0 + j
                trans(ptrF32[0:128, sl * 512 + j * 128:
                             sl * 512 + (j + 1) * 128],
                      xn_bf[par][:, kc * 128:(kc + 1) * 128], ident[:],
                      reads=(f"xn_bf{par}", "c.ident"),
                      writes=(f"ptr.s{sl}",))
            cp_s(xn_fm[:].rearrange("p (kc t) -> p kc t", kc=8)
                 [:, j0:j0 + 4, hc * 128:(hc + 1) * 128],
                 ptrF32[0:128, sl * 512:sl * 512 + 512]
                 .rearrange("p (kc t) -> p kc t", kc=4),
                 (f"ptr.s{sl}",),
                 tuple(f"xnfm.{kc}.{hc}" for kc in range(j0, j0 + 4)))

    def c_quant(hc):
        par = hc % 2
        t = hc - 1
        quant(xn_tm[par], xq_tm[par][:], 128, 1024,
              (f"xroll.{par}",), (f"xn_bf{par}",))
        if t % 2 == 1:
            pi = t // 2
            P.op("vector", lambda e,
                 o=p4[0:128, pi * 8:(pi + 1) * 8],
                 i=xn_fm[:].rearrange("p (kc c) -> p kc c", kc=8)
                 [:, :, t * 128:(t + 2) * 128]:
                 e.tensor_reduce(o, i, AX.X, ALU.add),
                 reads=tuple(f"xnfm.{kc}.{hc}" for kc in range(8)
                             for hc in (t, t + 1)),
                 writes=("p4",))

    def c_stage2(hc):
        par = hc % 2
        t = hc - 1
        for grp in range(2):
            j0, sl = grp * 4, grp
            for j in range(4):
                kc = j0 + j
                trans(ptrF32[0:128, sl * 512 + j * 128:
                             sl * 512 + (j + 1) * 128],
                      xq_tm[par][:, kc * 128:(kc + 1) * 128], ident[:],
                      reads=(f"xn_bf{par}", "c.ident"),
                      writes=(f"ptr.s{sl}",))
            cp_s(xqT_st[:, j0 * 128:(j0 + 4) * 128],
                 ptrF32[0:128, sl * 512:sl * 512 + 512],
                 (f"ptr.s{sl}",), ("xqT_st",))
        P.op("scalar", lambda e, o=xq_d[:, t * 1024:(t + 1) * 1024],
             i=xqT_st[:]: e.dma_start(out=o, in_=i),
             reads=("xqT_st",), writes=(f"xqd.{t}",), sem="d_xqst", inc=16)

    for i in range(NHC + 2):
        if 2 <= i <= NHC:
            c_quant(i - 1)
        if 3 <= i <= NHC + 1:
            c_stage2(i - 2)
        if i < NHC:
            c_stage1(i)

    act(pooled_bf[:].rearrange("p (kc pi) -> p kc pi", pi=8),
        p4[0:128, 0:64].rearrange("p (pi kc) -> p kc pi", kc=8),
        AF.Identity, ("p4",), ("pooled_bf",), scale=1.0 / 256.0)
    for mc in range(4):
        bank, bres = (psE, "E0") if mc % 2 == 0 else (psO, "O0")
        for kc in range(8):
            mm(bank[0:128, mc * 8:(mc + 1) * 8],
               rkvw_fm[:, kc * 512 + mc * 128:kc * 512 + (mc + 1) * 128],
               pooled_bf[:, kc * 8:(kc + 1) * 8],
               start=(kc == 0), stop=(kc == 7),
               reads=("pooled_bf", "w.rkv", wtok), writes=(bres,))
        P.op("scalar", lambda e, o=rkv_bf[:, mc * 8:(mc + 1) * 8],
             i=bank[0:128, mc * 8:(mc + 1) * 8], b=rkvb_fm[:, mc:mc + 1]:
             e.activation(o, i, AF.Identity, bias=b),
             reads=(bres, "c.rkvb"), writes=(f"rkvbf.{mc}",))
    for mc in range(4):
        trans(ptrF32[0:8, mc * 128:(mc + 1) * 128],
              rkv_bf[:, mc * 8:(mc + 1) * 8], ident[:],
              reads=(f"rkvbf.{mc}", "c.ident"), writes=("ptr.s0",))
    cp_s(contrib[:], ptrF32[0:8, 0:512], ("ptr.s0",),
         tuple(f"contrib.{mc}" for mc in range(4)))
    dma("gpsimd", rkv_c, contrib[:],
        tuple(f"contrib.{mc}" for mc in range(4)), ("rkv_c",))
    P.op("gpsimd", lambda e: e.collective_compute(
        "AllGather", ALU.bypass, replica_groups=RG,
        ins=[rkv_c.opt()], outs=[rkv_g.opt()]),
        reads=("rkv_c",), writes=("rkv_g",), sem="col_rkv", inc=1)

    esW.close()

    k_roll = sbB("k_roll", [128, 8 * 3 * 128], BF16)
    v_roll = sbB("v_roll", [128, 3 * 16 * 65], BF16)
    q_fm = sbB("q_fm", [128, 8 * 256], BF16)
    rq_fm = sbB("rq_fm", [128, 4 * 512], BF16)
    e_raw = [sbB(f"e_raw{i}", [128, 768], BF16) for i in range(4)]
    u_bf = sbB("u_bf", [128, 4 * 256], BF16)
    Dt = [sbB(f"Dt{i}", [128, 256], F32) for i in range(2)]
    rdt0 = sbB("rdt0", [128, 256], F32)
    z10 = sbB("z10", [128, 256], F32)
    w10 = sbB("w10", [128, 256], F32)
    rdt = [rdt0, rdt0]
    z1 = [z10, z10]
    w1 = [w10, w10]
    a4 = [sbB(f"a4{i}", [128, 256], BF16) for i in range(2)]
    b4 = [sbB(f"b4{i}", [128, 256], BF16) for i in range(2)]
    ones2 = sbB("ones2", [128, 64], BF16)
    lat_row = sbB("lat_row", [1, 512], F32)
    lat_rbf = sbB("lat_rbf", [1, 512], BF16)
    a_bf = sbB("a_bf", [128, 256], BF16)
    b_bf = sbB("b_bf", [128, 256], BF16)
    latout_fm = sbB("latout_fm", [128, 2 * 2048], BF16)
    remote_fm = sbB("remote_fm", [128, 8 * 256], BF16)
    mixed_fm = sbB("mixed_fm", [128, 8 * 256], BF16)

    memset(ones2[:], 1.0, ("c.ones2",))
    for i in range(2):
        memset(Dt[i][:], 0.0, (f"Dt{i}",))
        P.op("vector", lambda e, o=ptrF32[:, i * 512:(i + 1) * 512]:
             e.memset(o, 0.0), reads=(), writes=(f"ptr.s{i}",))

    def kv_bank(i):
        return (psE, "E0") if i % 2 == 0 else (psO, "O0")

    def kv_bank_hi(i):
        return ((psE[0:128, 512:1024], "E1") if i % 2 == 0
                else (psO[0:128, 512:1024], "O1"))

    def produce_k_one(hc):
        slot = hc % 3
        xn_r = tuple(f"xnfm.{kc}.{hc}" for kc in range(8))
        for p in range(8):
            bank, bres = kv_bank_hi(p)
            for kc in range(8):
                mm(bank[0:128, 0:128],
                   qkv_fm[:, kc * 3072 + 1024 + p * 128:
                          kc * 3072 + 1024 + (p + 1) * 128],
                   xn_fm[:, xcol(kc, hc):xcol(kc, hc) + 128],
                   start=(kc == 0), stop=(kc == 7),
                   reads=xn_r + ("w.qkv",), writes=(bres,))
            P.op("vector", lambda e,
                 o=k_roll[:, (p * 3 + slot) * 128:(p * 3 + slot) * 128 + 128],
                 i=bank[0:128, 0:128], b=kb_fm[:, p:p + 1]:
                 e.tensor_scalar(o, i, b, None, ALU.add),
                 reads=(bres, "c.kb"), writes=(f"kroll.{p}.{slot}",))

    def produce_k_pair(hc1):
        s1, s2 = hc1 % 3, (hc1 + 1) % 3
        xn_r = tuple(f"xnfm.{kc}.{hc}" for kc in range(8)
                     for hc in (hc1, hc1 + 1))
        for p in range(8):
            bank, bres = kv_bank_hi(p)
            for kc in range(8):
                mm(bank[0:128, 0:256],
                   qkv_fm[:, kc * 3072 + 1024 + p * 128:
                          kc * 3072 + 1024 + (p + 1) * 128],
                   xn_fm[:, xcol(kc, hc1):xcol(kc, hc1) + 256],
                   start=(kc == 0), stop=(kc == 7),
                   reads=xn_r + ("w.qkv",), writes=(bres,))
            for ci, slot in ((0, s1), (1, s2)):
                P.op("vector", lambda e,
                     o=k_roll[:, (p * 3 + slot) * 128:
                              (p * 3 + slot) * 128 + 128],
                     i=bank[0:128, ci * 128:(ci + 1) * 128],
                     b=kb_fm[:, p:p + 1]:
                     e.tensor_scalar(o, i, b, None, ALU.add),
                     reads=(bres, "c.kb"), writes=(f"kroll.{p}.{slot}",))

    def produce_v(hc):
        slot = hc % 3
        xn_r = tuple(f"xnfm.{kc}.{hc}" for kc in range(8))
        for vt in range(2):
            bank, bres = kv_bank_hi(vt)
            for kc in range(8):
                mm(bank[:, 0:512],
                   xn_fm[:, xcol(kc, hc):xcol(kc, hc) + 128],
                   qkv_fm[:, kc * 3072 + 2048 + vt * 512:
                          kc * 3072 + 2048 + (vt + 1) * 512],
                   start=(kc == 0), stop=False,
                   reads=xn_r + ("w.qkv",), writes=(bres,))
            mm(bank[:, 0:512], ones_r[0:1, 0:128],
               vb_row[0:1, vt * 512:(vt + 1) * 512],
               start=False, stop=True, reads=("c.ones", "c.vb"),
               writes=(bres,))
            P.op("vector", lambda e,
                 o=v_roll[:, (slot * 16 + vt * 8) * 65:
                          (slot * 16 + vt * 8 + 8) * 65].rearrange(
                     "p (h c) -> p h c", c=65)[:, :, 0:64],
                 i=bank[:, 0:512].rearrange("p (h c) -> p h c", c=64):
                 e.tensor_copy(o, i),
                 reads=(bres,), writes=(f"vroll.{vt}.{slot}",))

    def qblock(b):
        hcs = (2 * b, 2 * b + 1, 2 * b + 2)
        if b == 0:
            produce_k_one(0)
            produce_v(0)
        produce_k_pair(2 * b + 1)
        produce_v(2 * b + 1)
        produce_v(2 * b + 2)
        kv_r = tuple(f"kroll.{p}.{hc % 3}" for p in range(8) for hc in hcs) + \
            tuple(f"vroll.{vt}.{hc % 3}" for vt in range(2) for hc in hcs)
        for p in range(8):
            bank, bres = kv_bank_hi(p)
            for kc in range(8):
                mm(bank[0:128, 0:256],
                   qkv_fm[:, kc * 3072 + p * 128:kc * 3072 + (p + 1) * 128],
                   xn_fm[:, xcol(kc, 2 * b + 1):xcol(kc, 2 * b + 1) + 256],
                   start=(kc == 0), stop=(kc == 7),
                   reads=(f"xnfm.{kc}.{2 * b + 1}", f"xnfm.{kc}.{2 * b + 2}",
                          "w.qkv"), writes=(bres,))
            P.op("vector", lambda e, o=q_fm[:, p * 256:(p + 1) * 256],
                 i=bank[0:128, 0:256], b=qb_fm[:, p:p + 1]:
                 e.tensor_scalar(o, i, b, None, ALU.add),
                 reads=(bres, "c.qb"), writes=(f"qfm.{p}",))
        lat_r = tuple(f"latout.{lh}.{b // 2}" for lh in range(LH))
        for mc in range(8):
            bank, bres = kv_bank_hi(mc)
            for kc in range(2):
                mm(bank[0:128, 0:256],
                   routw_fm[:, kc * 1024 + mc * 128:kc * 1024 + (mc + 1) * 128],
                   latout_fm[:, kc * 2048 + b * 256:
                             kc * 2048 + (b + 1) * 256],
                   start=(kc == 0), stop=(kc == 1),
                   reads=lat_r + ("w.rout",), writes=(bres,))
            P.op("vector", lambda e, o=remote_fm[:, mc * 256:(mc + 1) * 256],
                 i=bank[0:128, 0:256], b=routb_fm[:, mc:mc + 1]:
                 e.tensor_scalar(o, i, b, None, ALU.add),
                 reads=(bres, "c.routb"), writes=(f"remote.{mc}",))
        mres = "c.mask0" if b == 0 else "c.maskg"
        moff = 0 if b == 0 else 768

        def head_scores(h):
            par = h % 2
            p = h // 2
            ph = par * 64
            psH = psE if par == 0 else psO
            bn0 = "E0" if par == 0 else "O0"
            bn1 = "E1" if par == 0 else "O1"
            for c3 in range(3):
                slot = hcs[c3] % 3
                dst = (psH[0:128, c3 * 256:(c3 + 1) * 256] if c3 < 2
                       else psH[0:128, 512:768])
                mm(dst,
                   k_roll[ph:ph + 64,
                          (p * 3 + slot) * 128:(p * 3 + slot) * 128 + 128],
                   q_fm[ph:ph + 64, p * 256:(p + 1) * 256],
                   start=True, stop=True,
                   reads=kv_r + (f"qfm.{p}",),
                   writes=(bn0 if c3 < 2 else bn1,))

        def head_expmask(h):
            par = h % 2
            e4 = h % 4
            psH = psE if par == 0 else psO
            bn0 = "E0" if par == 0 else "O0"
            bn1 = "E1" if par == 0 else "O1"
            er = e_raw[e4]
            act(er[:], psH[0:128, 0:768], AF.Exp, (bn0, bn1),
                (f"eraw{e4}",), scale=0.125)
            tt(er[:], er[:], masks[:, moff:moff + 768], ALU.mult,
               (f"eraw{e4}", mres), (f"eraw{e4}",))

        def head_U(h):
            par = h % 2
            p = h // 2
            ph = par * 64
            g = h // 4
            j = h % 4
            gp = g % 2
            pr = (h % 4) // 2
            psH = psE if par == 0 else psO
            bn1 = "E1" if par == 0 else "O1"
            pU = p4 if par == 0 else p5
            bnU = "p4" if par == 0 else "p5"
            er = e_raw[h % 4]
            ptrF = ptrF32[:, gp * 512:(gp + 1) * 512]
            for c3 in range(3):
                slot = hcs[c3] % 3
                mm(pU[0:65, 0:256],
                   v_roll[:, (slot * 16 + h) * 65:(slot * 16 + h) * 65 + 65],
                   er[:, c3 * 256:(c3 + 1) * 256],
                   start=(c3 == 0), stop=(c3 == 2),
                   reads=(f"eraw{h % 4}",) + kv_r + ("vroll.ones",),
                   writes=(bnU,))
            ub0 = (gp * 2 + pr) * 256
            cp_s(u_bf[ph:ph + 64, ub0:ub0 + 256],
                 pU[0:64, 0:256], (bnU,), (f"ubf.{par}.{gp}.{pr}",))
            act(Dt[gp][32 * j:32 * j + 1, 0:256], pU[64:65, 0:256], AF.Ln,
                (bnU, "c.cv1"), (f"Dt{gp}",), bias=cvec[0:1, 1:2])
            mm(ptrF[32 * j:32 * j + 1, 0:256], gl_col[ph:ph + 64, 0:1],
               u_bf[ph:ph + 64, ub0:ub0 + 256],
               start=True, stop=True,
               reads=(f"ubf.{par}.{gp}.{pr}", "c.gl"),
               writes=(f"ptr.s{gp}",), tp=(ph, 32 * j))
            mm(ptrF[32 * j:32 * j + 1, 256:512], gr2[:, par:par + 1],
               remote_fm[:, p * 256:(p + 1) * 256],
               start=True, stop=True, reads=(f"remote.{p}", "c.gr"),
               writes=(f"ptr.s{gp}",), tp=(0, 32 * j))

        def group_chain(g):
            gp = g % 2
            ptrF = ptrF32[:, gp * 512:(gp + 1) * 512]
            pres = f"ptr.s{gp}"
            dres = f"Dt{gp}"
            act(rdt[gp][:], Dt[gp][:], AF.Exp, (dres,), ("rdt",),
                scale=-1.0)
            P.op("vector", lambda e, o=z1[gp][:], i=rdt[gp][:],
                 r=ptrF[:, 0:256]: e.tensor_tensor(o, i, r, ALU.mult),
                 reads=("rdt", pres), writes=("z1",))
            P.op("vector", lambda e, o=z1[gp][:], i=z1[gp][:],
                 r=ptrF[:, 256:512]: e.tensor_tensor(o, i, r, ALU.add),
                 reads=("z1", pres), writes=("z1",))
            act(w1[gp][:], z1[gp][:], AF.Exp, ("z1",), ("w1",),
                scale=-1.0)
            act(rdt[gp][:], w1[gp][:], AF.Ln, ("w1", "c.cv2"),
                ("rdt",), bias=cvec[:, 2:3])
            tt(w1[gp][:], Dt[gp][:], rdt[gp][:], ALU.add,
               (dres, "rdt"), ("w1",))
            act(a4[gp][:], w1[gp][:], AF.Exp, ("w1",), (f"a4{gp}",),
                scale=-1.0)
            tt(w1[gp][:], z1[gp][:], rdt[gp][:], ALU.add,
               ("z1", "rdt"), ("w1",))
            act(b4[gp][:], w1[gp][:], AF.Exp, ("w1",), (f"b4{gp}",),
                scale=-1.0)

        def pair_mix(g, pr):
            gp = g % 2
            p = 2 * g + pr
            jA = 2 * pr
            jB = 2 * pr + 1
            psH = psE if pr == 0 else psO
            bn0 = "E0" if pr == 0 else "O0"
            bn1 = "E1" if pr == 0 else "O1"
            mm(psH[0:64, 768:1024], ones2[32 * jA:32 * jA + 1, 0:64],
               a4[gp][32 * jA:32 * jA + 1, 0:256],
               start=True, stop=True, reads=(f"a4{gp}", "c.ones2"),
               writes=(bn1,), tp=(32 * jA, 0))
            mm(psH[64:128, 768:1024], ones2[32 * jB:32 * jB + 1, 0:64],
               a4[gp][32 * jB:32 * jB + 1, 0:256],
               start=True, stop=True, reads=(f"a4{gp}", "c.ones2"),
               writes=(bn1,), tp=(32 * jB, 64))
            pz, bz = (p4, "p4") if pr == 0 else (p5, "p5")
            mm(pz[0:64, 256:512], ones2[32 * jA:32 * jA + 1, 0:64],
               b4[gp][32 * jA:32 * jA + 1, 0:256],
               start=True, stop=True, reads=(f"b4{gp}", "c.ones2"),
               writes=(bz,), tp=(32 * jA, 0))
            mm(pz[64:128, 256:512], ones2[32 * jB:32 * jB + 1, 0:64],
               b4[gp][32 * jB:32 * jB + 1, 0:256],
               start=True, stop=True, reads=(f"b4{gp}", "c.ones2"),
               writes=(bz,), tp=(32 * jB, 64))
            cp_s(a_bf[:], psH[:, 768:1024], (bn1,), ("abf",))
            cp_s(b_bf[:], pz[:, 256:512], (bz,), ("bbf",))
            P.op("vector", lambda e,
                 o=mixed_fm[:, p * 256:(p + 1) * 256],
                 i=u_bf[:, (gp * 2 + pr) * 256:(gp * 2 + pr + 1) * 256],
                 r=a_bf[:]:
                 e.tensor_tensor(o, i, r, ALU.mult),
                 reads=(f"ubf.0.{gp}.{pr}", f"ubf.1.{gp}.{pr}", "abf"),
                 writes=(f"mixed.{p}",))
            P.op("vector", lambda e, o=b_bf[:],
                 i=remote_fm[:, p * 256:(p + 1) * 256], r=b_bf[:]:
                 e.tensor_tensor(o, i, r, ALU.mult),
                 reads=(f"remote.{p}", "bbf"), writes=("bbf",))
            P.op("vector", lambda e,
                 o=mixed_fm[:, p * 256:(p + 1) * 256],
                 i=mixed_fm[:, p * 256:(p + 1) * 256], r=b_bf[:]:
                 e.tensor_tensor(o, i, r, ALU.add),
                 reads=(f"mixed.{p}", "bbf"), writes=(f"mixed.{p}",))

        for pi_ in range(10):
            if pi_ < 8:
                head_scores(2 * pi_)
                head_scores(2 * pi_ + 1)
            if pi_ < 8:
                head_expmask(2 * pi_)
                head_expmask(2 * pi_ + 1)
            if 1 <= pi_ <= 8:
                for h in (2 * pi_ - 2, 2 * pi_ - 1):
                    head_U(h)
                    if h % 4 == 3:
                        group_chain(h // 4)
            if pi_ >= 3 and pi_ % 2 == 1:
                g = (pi_ - 3) // 2
                pair_mix(g, 0)
                pair_mix(g, 1)
        mix_r = tuple(f"mixed.{p}" for p in range(8))
        for half in range(2):
            t = 2 * b + half
            dma("sync", x_res[:], x_d[(t + 1) * 128:(t + 2) * 128, :], (),
                ("x_res",) + (("lat_u",) if b == 0 and half == 0 else ()))
            for nt in range(2):
                bank, bres = kv_bank(nt)
                for kc in range(8):
                    mm(bank[:, 0:512],
                       mixed_fm[:, kc * 256 + half * 128:
                                kc * 256 + half * 128 + 128],
                       outw_fm[:, kc * 1024 + nt * 512:
                               kc * 1024 + (nt + 1) * 512],
                       start=(kc == 0), stop=False,
                       reads=mix_r + ("w.out",), writes=(bres,))
                mm(bank[:, 0:512], ones_r[0:1, 0:128],
                   outb_row[0:1, nt * 512:(nt + 1) * 512],
                   start=False, stop=True, reads=("c.ones", "c.outb"),
                   writes=(bres,))
                P.op("vector", lambda e, o=x_res[:, nt * 512:(nt + 1) * 512],
                     i=bank[:, 0:512], r=x_res[:, nt * 512:(nt + 1) * 512]:
                     e.tensor_tensor(o, i, r, ALU.add),
                     reads=(bres, "x_res"), writes=("x_res",))
            P.op("scalar", lambda e, o=y_d[t * 128:(t + 1) * 128, :],
                 i=x_res[:]: e.dma_start(out=o, in_=i),
                 reads=("x_res",), writes=(f"y1d.{t}",), sem="d_y1st",
                 inc=16)

    lat_u = x_res[0:64, 512:1024]
    def rq_emit(nc4):
        sl4 = nc4 % 2
        for mc in range(2):
            bank, bres = kv_bank(mc)
            for kc in range(8):
                mm(bank[0:128, 0:512],
                   rqw_fm[:, kc * 256 + mc * 128:kc * 256 + (mc + 1) * 128],
                   xn_fm[:, xcol(kc, 4 * nc4 + 1):xcol(kc, 4 * nc4 + 1) + 512],
                   start=(kc == 0), stop=(kc == 7),
                   reads=tuple(f"xnfm.{kc}.{hc}"
                               for hc in range(4 * nc4 + 1, 4 * nc4 + 5))
                   + ("w.rq",), writes=(bres,))
            P.op("vector", lambda e,
                 o=rq_fm[:, (sl4 * 2 + mc) * 512:(sl4 * 2 + mc + 1) * 512],
                 i=bank[0:128, 0:512], b=rqb_fm[:, mc:mc + 1]:
                 e.tensor_scalar(o, i, b, None, ALU.add),
                 reads=(bres, "c.rqb"), writes=(f"rqfm.{sl4}.{mc}",))

    rq_emit(0)
    rq_emit(1)
    dma("gpsimd", gath[:], rkv_g, ("rkv_g",), ("gath",))
    for lh in range(LH):
        trans(ptrF32[0:64, lh * 64:(lh + 1) * 64],
              gath[:, lh * 64:lh * 64 + 64],
              ident[0:64, 0:64],
              reads=("gath", "c.ident"), writes=("ptr.s0",))
    for lh in range(LH):
        cp_s(rk_fm[(lh % 2) * 64:(lh % 2) * 64 + 64,
                   (lh // 2) * 64:(lh // 2) * 64 + 64],
             ptrF32[0:64, lh * 64:(lh + 1) * 64],
             ("ptr.s0",), (f"rkfm.{lh}",))
        cp_v(rv_l[:, lh * 65:lh * 65 + 64],
             gath[:, 256 + lh * 64:256 + lh * 64 + 64], ("gath",),
             (f"rvl.{lh}",))
    memset(rv_l[:, 64::65], 1.0, ("rvl.ones",))
    memset(v_roll[:, 64::65], 1.0,
           ("vroll.ones", "gath") + tuple(f"contrib.{mc}" for mc in range(4)))

    for nc4 in range(4):
        for lh in range(LH):
            it = nc4 * LH + lh
            pb = (lh % 2) * 64
            sbank, sres = (psE, "E0") if it % 2 == 0 else (psO, "O0")
            pU, bnU = (p4, "p4") if it % 2 == 0 else (p5, "p5")
            pB, bnB = (p5, "p5") if it % 2 == 0 else (p4, "p4")
            le = e_raw[it % 2][0:64, 0:512]
            ler = f"eraw{it % 2}"
            mm(sbank[0:64, 0:512],
               rk_fm[pb:pb + 64, (lh // 2) * 64:(lh // 2) * 64 + 64],
               rq_fm[pb:pb + 64,
                     ((nc4 % 2) * 2 + lh // 2) * 512:
                     ((nc4 % 2) * 2 + lh // 2 + 1) * 512],
               start=True, stop=True,
               reads=(f"rkfm.{lh}", f"rqfm.{nc4 % 2}.{lh // 2}"),
               writes=(sres,))
            act(le, sbank[0:64, 0:512], AF.Exp, (sres,), (ler,), scale=0.125)
            tt(le, le, masklat[:, nc4 * 512:(nc4 + 1) * 512], ALU.mult,
               (ler, "c.masklat"), (ler,))
            mm(pU[0:65, 0:512], rv_l[:, lh * 65:lh * 65 + 65], le,
               start=True, stop=True,
               reads=(ler, f"rvl.{lh}", "rvl.ones"), writes=(bnU,))
            act(lat_row[0:1, 0:512], pU[64:65, 0:512], AF.Ln,
                (bnU, "c.cv1"), ("lrow.0",), bias=cvec[0:1, 1:2])
            act(lat_row[0:1, 0:512], lat_row[0:1, 0:512], AF.Exp,
                ("lrow.0",), ("lrow.0",), scale=-1.0)
            cp_v(lat_rbf[0:1, 0:512], lat_row[0:1, 0:512], ("lrow.0",),
                 ("lrbf",))
            mm(pB[0:64, 0:512], ones_r[0:1, 0:64], lat_rbf[0:1, 0:512],
               start=True, stop=True, reads=("lrbf", "c.ones"),
               writes=(bnB,))
            cp_s(lat_u[:, 0:512], pU[0:64, 0:512], (bnU,), ("lat_u",))
            P.op("vector", lambda e,
                 o=latout_fm[pb:pb + 64,
                             (lh // 2) * 2048 + nc4 * 512:
                             (lh // 2) * 2048 + (nc4 + 1) * 512],
                 i=lat_u[:, 0:512], r=pB[0:64, 0:512]:
                 e.tensor_tensor(o, i, r, ALU.mult),
                 reads=("lat_u", bnB), writes=(f"latout.{lh}.{nc4}",))
        if nc4 + 2 < 4:
            rq_emit(nc4 + 2)

    for b in range(NQB):
        qblock(b)

    P.barrier()
    esB.close()

    esA = ExitStack()

    def sbA(name, shape, dt, **kw):
        return esA.enter_context(nc.sbuf_tensor(name, list(shape), dt, **kw))

    wgT = sbA("wgT", [128, 8 * FFI], BF16)
    wuT = sbA("wuT", [128, 8 * FFI], BF16)
    wdT = sbA("wdT", [128, 22 * 1024], BF16)
    xq_t = [sbA(f"xq_t{i}", [128, 1024], BF16) for i in range(2)]
    h_tm = [sbA(f"h_tm{i}", [128, 1024], F32) for i in range(4)]
    h_q = [sbA(f"h_q{i}", [128, 1024], BF16) for i in range(2)]
    hqT = [sbA(f"hqT{i}", [128, 1024], BF16) for i in range(2)]
    y1_in = [sbA(f"y1_in{i}", [128, 1024], F32) for i in range(2)]
    y_out = [sbA(f"y_out{i}", [128, 1024], F32) for i in range(2)]

    for t0 in range(2):
        P.op("sync", lambda e, o=y1_in[t0][:],
             i=y_d[t0 * 128:(t0 + 1) * 128, :]: e.dma_start(out=o, in_=i),
             reads=(f"y1d.{t0}",), writes=(f"y1_in{t0}",),
             sem=f"d_ya{t0}", inc=16)
        P.op("sync", lambda e, o=xq_t[t0][:],
             i=xq_d[:, t0 * 1024:(t0 + 1) * 1024]: e.dma_start(out=o, in_=i),
             reads=(f"xqd.{t0}",), writes=(f"xq_t{t0}",),
             sem=f"d_xa{t0}", inc=16)
    for r in range(NCORES):
        g = r // 3
        P.op("sync", lambda e,
             o=wgT[:].rearrange("p (kc n) -> p kc n", kc=8)
             [:, :, r * 352:(r + 1) * 352],
             i=wgu_g[r * 2048:r * 2048 + 1024, :].rearrange(
                 "(kc p) n -> p kc n", p=128):
             e.dma_start(out=o, in_=i),
             reads=("wgu_g",), writes=(f"wgT.{r}",),
             sem=f"d_wg{g}", inc=16, serial=False)
        P.op("gpsimd", lambda e,
             o=wuT[:].rearrange("p (kc n) -> p kc n", kc=8)
             [:, :, r * 352:(r + 1) * 352],
             i=wgu_g[r * 2048 + 1024:r * 2048 + 2048, :].rearrange(
                 "(kc p) n -> p kc n", p=128):
             e.dma_start(out=o, in_=i),
             reads=("wgu_g",), writes=(f"wuT.{r}",),
             sem=f"d_wu{g}", inc=16, serial=False)
        P.op("scalar", lambda e,
             o=wdT[:].rearrange("p (kc n) -> p kc n", kc=22)
             [:, :, r * 128:(r + 1) * 128],
             i=wd_g[r * FFI:(r + 1) * FFI, :].rearrange(
                 "(kc p) n -> p kc n", p=128):
             e.dma_start(out=o, in_=i),
             reads=("wd_g",), writes=(f"wdT.{r}",),
             sem="d_wd", inc=16, serial=False)

    GLAST = (2, 5, 7)

    def rtok(pfx, n0, nw):
        g = ((n0 + nw - 1) // 352) // 3
        return tuple(f"{pfx}.{GLAST[gi]}" for gi in range(g + 1))

    SLICES = ((0, 1024), (1024, 1024), (2048, 768))
    slices = [(t, si, c0, w) for t in range(NCH)
              for si, (c0, w) in enumerate(SLICES)]

    def a_stage1a(i):
        t, si, c0, w = slices[i]
        tp_ = t % 2
        if si == 0 and t >= 2:
            P.op("sync", lambda e, o=y1_in[tp_][:],
                 i=y_d[t * 128:(t + 1) * 128, :]: e.dma_start(out=o, in_=i),
                 reads=(f"y1d.{t}",), writes=(f"y1_in{tp_}",),
                 sem=f"d_ya{tp_}", inc=16)
            P.op("sync", lambda e, o=xq_t[tp_][:],
                 i=xq_d[:, t * 1024:(t + 1) * 1024]: e.dma_start(out=o, in_=i),
                 reads=(f"xqd.{t}",), writes=(f"xq_t{tp_}",),
                 sem=f"d_xa{tp_}", inc=16)
        hp = i % 4
        for wT, pb, res0, res1, rr in ((wgT, psE, "E0", "E1", "wgT"),
                                       (wuT, psO, "O0", "O1", "wuT")):
            for kc in range(8):
                mm(pb[0:128, 0:512],
                   xq_t[tp_][:, kc * 128:(kc + 1) * 128],
                   wT[:, kc * FFI + c0:kc * FFI + c0 + 512],
                   start=(kc == 0), stop=(kc == 7),
                   reads=rtok(rr, c0, 512) + (f"xq_t{tp_}",),
                   writes=(res0,))
                mm(pb[0:128, 512:w],
                   xq_t[tp_][:, kc * 128:(kc + 1) * 128],
                   wT[:, kc * FFI + c0 + 512:kc * FFI + c0 + w],
                   start=(kc == 0), stop=(kc == 7),
                   reads=rtok(rr, c0, w) + (f"xq_t{tp_}",),
                   writes=(res1,))
        act(h_tm[hp][:, 0:w], psE[0:128, 0:w], AF.Silu, ("E0", "E1"),
            (f"hs{hp}",))

    def a_stage1b(i):
        t, si, c0, w = slices[i]
        hp = i % 4
        P.op("vector", lambda e, o=h_tm[hp][:, 0:w],
             i=h_tm[hp][:, 0:w], r=psO[0:128, 0:w]:
             e.tensor_tensor(o, i, r, ALU.mult),
             reads=(f"hs{hp}", "O0", "O1"),
             writes=(f"h{hp}.0", f"h{hp}.512"))

    def a_quant(i, half):
        t, si, c0, w = slices[i]
        par = i % 2
        hp = i % 4
        h0 = half * 512
        hw = 512 if half == 0 else w - 512
        quant(h_tm[hp][:, h0:h0 + hw], h_q[par][:, h0:h0 + hw], 128, hw,
              (f"h{hp}.{h0}",), (f"hq{par}",))

    def a_stage2(i):
        t, si, c0, w = slices[i]
        tp_ = t % 2
        par = i % 2
        nkc = w // 128
        for grp in range((nkc + 3) // 4):
            j0 = grp * 4
            j1 = min(j0 + 4, nkc)
            sl = grp % 2
            for j in range(j0, j1):
                trans(ptrF32[0:128, sl * 512 + (j - j0) * 128:
                             sl * 512 + (j - j0 + 1) * 128],
                      h_q[par][:, j * 128:(j + 1) * 128], ident[:],
                      reads=(f"hq{par}", "c.ident"), writes=(f"ptr.s{sl}",))
            cp_s(hqT[par][:, j0 * 128:j1 * 128],
                 ptrF32[0:128, sl * 512:sl * 512 + (j1 - j0) * 128],
                 (f"ptr.s{sl}",), (f"hqT{par}",))
        for j in range(nkc):
            kc = c0 // 128 + j
            for nt in range(2):
                bank, bres = (p4, "p4") if nt == 0 else (p5, "p5")
                mm(bank[:, 0:512],
                   hqT[par][:, j * 128:(j + 1) * 128],
                   wdT[:, kc * 1024 + nt * 512:kc * 1024 + (nt + 1) * 512],
                   start=(kc == 0), stop=(kc == 21),
                   reads=("wdT.7", f"hqT{par}"), writes=(bres,))
        if si == 2:
            for nt in range(2):
                bank, bres = (p4, "p4") if nt == 0 else (p5, "p5")
                P.op("vector", lambda e,
                     o=y_out[tp_][:, nt * 512:(nt + 1) * 512],
                     i=bank[:, 0:512],
                     r=y1_in[tp_][:, nt * 512:(nt + 1) * 512]:
                     e.tensor_tensor(o, i, r, ALU.add),
                     reads=(bres, f"y1_in{tp_}"), writes=(f"yo{tp_}.{nt}",))
            P.op("scalar", lambda e, o=y_d[t * 128:(t + 1) * 128, :],
                 i=y_out[tp_][:]: e.dma_start(out=o, in_=i),
                 reads=(f"yo{tp_}.0", f"yo{tp_}.1"), writes=(f"yfinal.{t}",),
                 sem=f"d_yf{tp_}", inc=16)

    for i in range(len(slices) + 2):
        if 1 <= i <= len(slices):
            a_quant(i - 1, 0)
        if i < len(slices):
            a_stage1a(i)
        if 1 <= i <= len(slices):
            a_quant(i - 1, 1)
        if i < len(slices):
            a_stage1b(i)
        if i >= 2:
            a_stage2(i - 2)

    P.barrier()
    esA.close()

    sems = {}
    sem_stack = ExitStack()
    for name in sorted(P.sems):
        sems[name] = sem_stack.enter_context(nc.semaphore(name))
    with nc.Block() as block:
        P.emit_all(block, sems)
    sem_stack.close()
    es.close()
    return nc


_NC_CACHE = {}


def kernel(L0_attn_x, L0_attn_qkv_w, L0_attn_qkv_b, L0_attn_rq_w, L0_attn_rq_b,
           L0_attn_rkv_w, L0_attn_rkv_b, L0_attn_rout_w, L0_attn_rout_b,
           L0_attn_out_w, L0_attn_out_b, L0_attn_gl_w, L0_attn_gr_w,
           L0_ffn_w_gate, L0_ffn_w_up, L0_ffn_w_down):
    f32 = np.float32
    x = np.ascontiguousarray(np.asarray(L0_attn_x, f32).reshape(S, D))
    qkv_w = np.ascontiguousarray(np.asarray(L0_attn_qkv_w, f32))
    qkv_b = np.asarray(L0_attn_qkv_b, f32)
    rq_w = np.ascontiguousarray(np.asarray(L0_attn_rq_w, f32))
    rq_b = np.asarray(L0_attn_rq_b, f32)
    rkv_w = np.ascontiguousarray(np.asarray(L0_attn_rkv_w, f32))
    rkv_b = np.asarray(L0_attn_rkv_b, f32)
    rout_w = np.ascontiguousarray(np.asarray(L0_attn_rout_w, f32))
    rout_b = np.asarray(L0_attn_rout_b, f32)
    out_w = np.ascontiguousarray(np.asarray(L0_attn_out_w, f32))
    out_b = np.asarray(L0_attn_out_b, f32)
    gl_w = np.asarray(L0_attn_gl_w, f32)
    gr_w = np.asarray(L0_attn_gr_w, f32)
    w_gate = np.ascontiguousarray(np.asarray(L0_ffn_w_gate, f32))
    w_up = np.ascontiguousarray(np.asarray(L0_ffn_w_up, f32))
    w_down = np.ascontiguousarray(np.asarray(L0_ffn_w_down, f32))

    def fm(vec, cols):
        return np.ascontiguousarray(vec.reshape(cols, 128).T)

    gl_col = np.ascontiguousarray(np.concatenate([gl_w[:, 0], gl_w[:, 0]]).reshape(128, 1))
    gr2 = np.zeros((128, 2), f32)
    gr2[0:64, 0] = gr_w[:, 0]
    gr2[64:128, 1] = gr_w[:, 0]
    ident = np.eye(128, dtype=BFnp)
    maskg = make_local_mask(False)
    mask0c0 = make_local_mask(True)

    nc = _NC_CACHE.get("nc")
    if nc is None:
        nc = build_program()
        _NC_CACHE["nc"] = nc
    in_maps = []
    for c in range(NCORES):
        lo = c * T - 128
        xh = np.zeros((NHC * 128, D), f32)
        if lo < 0:
            xh[128:] = x[0:T]
        else:
            xh[:] = x[lo:(c + 1) * T]
        in_maps.append({
            "x": xh,
            "qkv_w": qkv_w, "rq_w": rq_w, "rkv_w": rkv_w,
            "rout_w": rout_w, "out_w": out_w,
            "gl_col": gl_col, "gr2": gr2,
            "qb_fm": fm(qkv_b[0:1024], 8), "kb_fm": fm(qkv_b[1024:2048], 8),
            "vb_row": np.ascontiguousarray(qkv_b[2048:3072].reshape(1, D)),
            "rqb_fm": fm(rq_b, 2), "rkvb_fm": fm(rkv_b, 4),
            "routb_fm": fm(rout_b, 8),
            "outb_row": np.ascontiguousarray(out_b.reshape(1, D)),
            "wg_sh": np.ascontiguousarray(w_gate[c * 352:(c + 1) * 352]),
            "wu_sh": np.ascontiguousarray(w_up[c * 352:(c + 1) * 352]),
            "wd_sh": np.ascontiguousarray(w_down[c * 128:(c + 1) * 128]),
            "mask0": mask0c0 if c == 0 else maskg,
            "maskg": maskg,
            "mask_lat": make_lat_mask(c),
            "ident": ident,
        })
    res = run_bass_kernel_spmd(nc, in_maps, core_ids=list(range(NCORES)))
    y = np.concatenate([np.asarray(r["y"], np.float32)
                        for r in res.results], axis=0)
    return y.reshape(1, S, D)
